# version 27
# baseline (speedup 1.0000x reference)
import os
import numpy as np
import ml_dtypes
from contextlib import ExitStack

RACE_SAFE = os.environ.get("BASS_SIM_FP64") == "1"
_OWN_SEM = {"vector": "v", "scalar": "s", "tensor": "t"}

import concourse.bass as bass
import concourse.mybir as mybir
from concourse.bass_utils import run_bass_kernel_spmd

F32 = mybir.dt.float32
BF16 = mybir.dt.bfloat16
AF = mybir.ActivationFunctionType
ALU = mybir.AluOpType
AX = mybir.AxisListType

B = 8
T = 4096
D = 1024
FF = 4096
NC = D // 128
FC = FF // 128
NT = T // 512
NQ = T // 128
H = 16
HD = 64

CC_RANKS = None

M1 = float(1.5 * 2.0 ** 23)
M2 = float(1.5 * 2.0 ** 41)
M3 = float(1.5 * 2.0 ** 52)
FMAX = float(np.finfo(np.float32).max)

ENGS = ("sync", "gpsimd", "vector", "scalar", "tensor")

OFF_A = 0
OFF_B = 131072
OFF_S = 196608
AR_BYTES = 204032


class Sched:
    def __init__(self, nc, es):
        self.nc = nc
        self.es = es
        self.streams = {e: [] for e in ENGS}
        self.sem = {}
        for name in ("t", "v", "s", "cc"):
            self.sem[name] = es.enter_context(nc.semaphore("sem_" + name))
        self.cnt = {k: 0 for k in self.sem}

    def get_sem(self, name):
        if name not in self.sem:
            self.sem[name] = self.es.enter_context(
                self.nc.semaphore("sem_" + name))
            self.cnt[name] = 0
        return self.sem[name]

    def op(self, eng, fn, inc=None):
        if eng == "vector" and inc is None:
            inc = "v"
        if eng == "scalar" and inc is None:
            inc = "s"
        if eng == "vector" and self.cnt["v"] > 0:
            self.wait(eng, ("v", self.cnt["v"]))
        elif eng == "scalar" and self.cnt["s"] > 0:
            if RACE_SAFE:
                self.wait(eng, ("s", self.cnt["s"]))
        if inc is None:
            self.streams[eng].append(lambda e, fn=fn: fn(e))
            return None
        amount = 1 if inc in ("t", "v", "s", "cc") else 16
        h = self.get_sem(inc)
        def g(e, fn=fn, h=h, amount=amount):
            fn(e).then_inc(h, amount)
        self.streams[eng].append(g)
        self.cnt[inc] += amount
        return (inc, self.cnt[inc])

    def wait(self, eng, tok):
        if tok is None:
            return
        if isinstance(tok, list):
            self.wait_all(eng, tok)
            return
        name, val = tok
        h = self.get_sem(name)
        self.streams[eng].append(lambda e, h=h, val=val: e.wait_ge(h, val))

    def wait_all(self, eng, toks):
        best = {}
        def visit(tk):
            if tk is None:
                return
            if isinstance(tk, list):
                for t2 in tk:
                    visit(t2)
                return
            name, val = tk
            best[name] = max(best.get(name, 0), val)
        for tk in toks:
            visit(tk)
        for name, val in best.items():
            h = self.get_sem(name)
            self.streams[eng].append(lambda e, h=h, val=val: e.wait_ge(h, val))

    def dma(self, queue, out, in_, stream="misc", transpose=False):
        def fn(e, out=out, in_=in_, transpose=transpose):
            return e.dma_start(out=out, in_=in_, transpose=transpose)
        return self.op(queue, fn, inc="D" + stream)

    def barrier_tokens(self):
        return [(k, v) for k, v in self.cnt.items() if v > 0]

    def full_barrier(self):
        toks = self.barrier_tokens()
        for e in ENGS:
            self.wait_all(e, toks)

    def emit(self, block):
        for e in ENGS:
            stream = self.streams[e]
            if not stream:
                continue
            def runner(eng, stream=stream):
                for fn in stream:
                    fn(eng)
            getattr(block, e)(runner)


class Ring:
    def __init__(self, views):
        self.views = views
        self.n = len(views)
        self.free = [None] * self.n
        self.k = 0

    def take(self):
        i = self.k % self.n
        self.k += 1
        return i, self.views[i], self.free[i]

    def release(self, i, tok):
        self.free[i] = tok


class Arena:
    def __init__(self, handle):
        self.f32 = handle.ap()
        self.bf16 = self.f32.bitcast(BF16)

    def v(self, byte_off, n, dtype, p0=0, p1=128):
        if dtype == F32:
            assert byte_off % 4 == 0 and byte_off + 4 * n <= AR_BYTES
            return self.f32[p0:p1, byte_off // 4: byte_off // 4 + n]
        assert byte_off % 2 == 0 and byte_off + 2 * n <= AR_BYTES
        return self.bf16[p0:p1, byte_off // 2: byte_off // 2 + n]



def _rope_tables():
    import jax
    import jax.numpy as jnp
    cpu = jax.local_devices(backend="cpu")[0]
    with jax.default_device(cpu):
        i = jnp.arange(D // 2, dtype=jnp.float32)
        phi = jnp.power(10000.0, -2.0 * (i - 1.0) / D)
        pos = jnp.arange(T, dtype=jnp.float32)
        ang = pos[:, None] * phi[None, :]
        cos = np.asarray(jnp.cos(ang), dtype=np.float32)
        sin = np.asarray(jnp.sin(ang), dtype=np.float32)
    return cos, sin


def _tile_block(w):
    ri, co = w.shape
    return np.ascontiguousarray(
        w.reshape(ri // 128, 128, co // 128, 128).transpose(0, 2, 1, 3))


def _host_prep(inputs):
    f32 = np.float32
    x = np.asarray(inputs["L0_attn_a_x"], dtype=f32)
    wq = np.asarray(inputs["L0_attn_a_wq"], dtype=f32)
    wk = np.asarray(inputs["L0_attn_a_wk"], dtype=f32)
    wv = np.asarray(inputs["L0_attn_a_wv"], dtype=f32)
    wqkv = np.asarray(inputs["L0_attn_b_w_qkv"], dtype=f32)
    wout = np.asarray(inputs["L0_attn_b_w_out"], dtype=f32)
    bout = np.asarray(inputs["L0_attn_b_b_out"], dtype=f32)
    w_in = np.asarray(inputs["L0_ffn_W_in"], dtype=f32)
    b_in = np.asarray(inputs["L0_ffn_b_in"], dtype=f32)
    w_out2 = np.asarray(inputs["L0_ffn_W_out"], dtype=f32)
    b_out2 = np.asarray(inputs["L0_ffn_b_out"], dtype=f32)

    perm = np.concatenate([np.arange(0, D, 2), np.arange(1, D, 2)])
    wqkvT = np.ascontiguousarray(wqkv.T)

    shared = {
        "wqTt": _tile_block(np.ascontiguousarray(wq[perm, :].T)),
        "wkTt": _tile_block(np.ascontiguousarray(wk[perm, :].T)),
        "wvTc": np.ascontiguousarray(wv.T.reshape(NC, 128, D)),
        "wqkvTqt": _tile_block(np.ascontiguousarray(wqkvT[:, 0:D])),
        "wqkvTkv": np.ascontiguousarray(wqkvT[:, D:3 * D].reshape(NC, 128, 2 * D)),
        "woutTc": np.ascontiguousarray(wout.T.reshape(NC, 128, D)),
        "w_inT": np.ascontiguousarray(w_in.T.reshape(FC, 128, D)),
        "w_out2": np.ascontiguousarray(w_out2.reshape(FC, 128, D)),
        "b_in": np.ascontiguousarray(b_in.reshape(FC, 128, 1)),
        "b_out2": np.ascontiguousarray(b_out2.reshape(NC, 128, 1)),
        "boutB": np.ascontiguousarray(bout.reshape(NC, 128, 1)),
    }

    cos, sin = _rope_tables()
    shared["cosT"] = np.ascontiguousarray(cos.T)
    shared["sinT"] = np.ascontiguousarray(sin.T)

    jj = np.arange(896)[None, :]
    qq = np.arange(128)[:, None]
    shared["maskstrip"] = np.where(jj <= 384 + qq, FMAX, -FMAX).astype(f32)
    shared["ident"] = np.eye(128, dtype=f32).astype(ml_dtypes.bfloat16)

    ind = np.zeros((H, NC * 128), dtype=f32)
    for c in range(NC):
        ind[2 * c, c * 128:c * 128 + 64] = 1.0
        ind[2 * c + 1, c * 128 + 64:c * 128 + 128] = 1.0
    shared["ind"] = np.ascontiguousarray(ind)

    cores = []
    for i in range(B):
        m = dict(shared)
        m["x_in"] = np.ascontiguousarray(x[i].T)
        cores.append(m)
    return cores



def build_program(stop_after="all"):
    nc = bass.Bass("TRN2", target_bir_lowering=False)

    dd = {}
    def din(name, shape, dtype=F32):
        dd[name] = nc.dram_tensor(name, shape, dtype, kind="ExternalInput").ap()
    din("x_in", [D, T])
    din("wqTt", [NC, NC, 128, 128])
    din("wkTt", [NC, NC, 128, 128])
    din("wvTc", [NC, 128, D])
    din("wqkvTqt", [NC, NC, 128, 128])
    din("wqkvTkv", [NC, 128, 2 * D])
    din("woutTc", [NC, 128, D])
    din("w_inT", [FC, 128, D])
    din("w_out2", [FC, 128, D])
    din("b_in", [FC, 128, 1])
    din("b_out2", [NC, 128, 1])
    din("boutB", [NC, 128, 1])
    din("cosT", [D // 2, T])
    din("sinT", [D // 2, T])
    din("maskstrip", [128, 896])
    din("ident", [128, 128], BF16)
    din("ind", [H, NC * 128])
    dd["y_out"] = nc.dram_tensor("y_out", [D, T], F32, kind="ExternalOutput").ap()

    dd["x_state"] = nc.dram_tensor("x_state", [NC, 128, T], F32).ap()
    dd["qr_sp"] = nc.dram_tensor("qr_sp", [NC, 128, T], BF16).ap()
    dd["w1p"] = nc.dram_tensor("w1p", [FC, NC, 128, 128], BF16).ap()
    dd["qwout"] = nc.dram_tensor("qwout", [FC, 128, D], BF16).ap()
    dd["hid"] = nc.dram_tensor("hid", [FC, NT, 128, 512], BF16).ap()
    dd["cc"] = [(nc.dram_tensor(f"cc_in{i}", [1, 8], F32).ap(),
                 nc.dram_tensor(f"cc_out{i}", [1, 8], F32).ap()) for i in range(4)]

    with ExitStack() as es:
        ar = Arena(es.enter_context(nc.sbuf_tensor("arena", [128, AR_BYTES // 4], F32)))
        smalls = es.enter_context(nc.sbuf_tensor("smalls", [128, 208], F32)).ap()
        sb_ident = es.enter_context(nc.sbuf_tensor("sb_ident", [128, 128], BF16)).ap()
        sb_onesb = es.enter_context(nc.sbuf_tensor("sb_onesb", [128, 8], BF16)).ap()
        sb_onesrow = es.enter_context(nc.sbuf_tensor("sb_onesrow", [1, 128], F32)).ap()
        sb_blk = es.enter_context(nc.sbuf_tensor("sb_blk", [128, H * NC], BF16)).ap()
        sb_kv = es.enter_context(nc.sbuf_tensor("sb_kv", [128, D], BF16)).ap()
        sb_invw = es.enter_context(nc.sbuf_tensor("sb_invw", [128, 2 * FC], F32)).ap()
        sb_bin = es.enter_context(nc.sbuf_tensor("sb_bin", [128, FC], F32)).ap()
        sb_bout2 = es.enter_context(nc.sbuf_tensor("sb_bout2", [128, NC], F32)).ap()
        sb_boutB = es.enter_context(nc.sbuf_tensor("sb_boutB", [128, NC], F32)).ap()
        sb_vecs = es.enter_context(nc.sbuf_tensor("sb_vecs", [128, 64], F32)).ap()
        sb_ind = es.enter_context(nc.sbuf_tensor("sb_ind", [H, NC * 128], F32)).ap()
        psum_h = es.enter_context(nc.psum_tensor("psum", [128, 4096], F32))
        psum = psum_h.ap()
        psum_bf = psum.bitcast(BF16)

        sch = Sched(nc, es)
        for ci_, val in enumerate((M2, -M2, M3, -M3, 1e-5, 4.844187086458591)):
            nc.const_aps.register(F32, val, sb_vecs[:, 40 + ci_:41 + ci_])
        ctx = dict(nc=nc, sch=sch, ar=ar, psum=psum, psum_bf=psum_bf, smalls=smalls,
                   ident=sb_ident, onesb=sb_onesb, onesrow=sb_onesrow,
                   blk=sb_blk, kvsb=sb_kv, invw=sb_invw, binsb=sb_bin,
                   bout2sb=sb_bout2, boutBsb=sb_boutB,
                   vecs=sb_vecs, indsb=sb_ind, d=dd)

        emit_all(ctx, stop_after)

        with nc.Block() as block:
            sch.emit(block)
    return nc


def _chunks(ap3):
    return [ap3[c] for c in range(NC)]


def emit_all(ctx, stop_after):
    sch, d = ctx["sch"], ctx["d"]
    x_in_chunks = [d["x_in"][c * 128:(c + 1) * 128, :] for c in range(NC)]
    xs_chunks = _chunks(d["x_state"])
    y_chunks = [d["y_out"][c * 128:(c + 1) * 128, :] for c in range(NC)]

    emit_setup(ctx)
    emit_wquant(ctx)
    sch.full_barrier()
    emit_ln(ctx, src=x_in_chunks, amax_col=None)
    sch.full_barrier()
    if stop_after == "ln1":
        emit_debug_xn(ctx)
        return
    emit_attna(ctx, x_in_chunks, xs_chunks)
    sch.full_barrier()
    if stop_after == "attna":
        emit_debug_hbm(ctx, xs_chunks)
        return
    emit_ln(ctx, src=xs_chunks, amax_col=0)
    sch.full_barrier()
    emit_ffn(ctx, app=0, xsrc=xs_chunks, xdst=xs_chunks)
    sch.full_barrier()
    if stop_after == "ffn1":
        emit_debug_hbm(ctx, xs_chunks)
        return
    emit_ln(ctx, src=xs_chunks, amax_col=None)
    sch.full_barrier()
    if stop_after == "ln3":
        emit_debug_xn(ctx)
        return
    emit_attnb(ctx, xs_chunks, xs_chunks)
    sch.full_barrier()
    if stop_after == "attnb":
        emit_debug_hbm(ctx, xs_chunks)
        return
    emit_ln(ctx, src=xs_chunks, amax_col=0)
    sch.full_barrier()
    emit_ffn(ctx, app=1, xsrc=xs_chunks, xdst=y_chunks)
    sch.full_barrier()


def emit_setup(ctx):
    sch, d = ctx["sch"], ctx["d"]
    for ci_, val in enumerate((M2, -M2, M3, -M3, 1e-5, 4.844187086458591)):
        sch.op("vector", lambda e, ap=ctx["vecs"][:, 40 + ci_:41 + ci_], v=val:
               e.memset(ap, v))
    ctx["ident_tok"] = sch.dma("sync", ctx["ident"], d["ident"], stream="ident")
    sch.dma("sync", ctx["indsb"], d["ind"])
    for c in range(FC):
        sch.dma("sync", ctx["binsb"][:, c:c + 1], d["b_in"][c])
    for c in range(NC):
        sch.dma("sync", ctx["bout2sb"][:, c:c + 1], d["b_out2"][c])
        sch.dma("sync", ctx["boutBsb"][:, c:c + 1], d["boutB"][c])
    sch.op("vector", lambda e, ap=ctx["onesb"]: e.memset(ap, 1.0))
    sch.op("vector", lambda e, ap=ctx["onesrow"]: e.memset(ap, 1.0))
    sch.op("vector", lambda e, ap=ctx["blk"]: e.memset(ap, 0.0), inc="v")


def emit_wquant(ctx):
    sch, ar, d = ctx["sch"], ctx["ar"], ctx["d"]
    A = OFF_A
    NDEEP = 3
    inring = Ring([ar.v(A + i * 4096, D, F32) for i in range(4)])
    amg = ar.v(A + 16384, 64, F32)
    scg = ar.v(A + 16384 + 256, 64, F32)
    ubs = [ar.v(A + 20480 + i * 4096, D, F32) for i in range(NDEEP)]
    tas = [ar.v(A + 32768 + i * 4096, D, F32) for i in range(NDEEP)]
    tms = [ar.v(A + 45056 + i * 4096, D, F32) for i in range(NDEEP)]
    qb = Ring([ar.v(A + 57344 + i * 2048, D, BF16) for i in range(NDEEP)])
    tstage = Ring([ar.v(A + 63488 + i * 1024, 512, BF16) for i in range(4)])

    sch.wait("tensor", ctx["ident_tok"])
    LN127 = 4.844187086458591
    ub_free = [None] * NDEEP
    tbank_free = [None, None]
    for w2 in range(2):
        src2 = d["w_inT"] if w2 == 0 else d["w_out2"]
        for c in range(FC):
            pp = (w2 * FC + c) % NDEEP
            gidx = w2 * 32 + c
            ri, rv, rf = inring.take()
            q = ("sync", "gpsimd")[ri % 2]
            sch.wait(q, rf)
            ld = sch.dma(q, rv, src2[c], stream=f"in{ri}")
            ga = amg[:, gidx:gidx + 1]
            gs = scg[:, gidx:gidx + 1]
            sch.wait("vector", ld)
            sch.op("vector", lambda e, o=ga, i=rv: e.reduce_max(
                o, i, axis=AX.X, apply_absolute_value=True))
            r2 = sch.op("vector", lambda e, o=ga, i=ga: e.tensor_scalar(
                o, i, 1e-8, None, ALU.max))
            r3 = sch.op("vector", lambda e, o=ctx["invw"][:, gidx:gidx + 1],
                        i=ga: e.tensor_scalar(o, i, 1.0 / 127.0, None, ALU.mult))
            sch.wait("scalar", r2)
            sch.op("scalar", lambda e, o=gs, i=ga: e.activation(
                o, i, AF.Ln, bias=0.0, scale=1.0))
            s1 = sch.op("scalar", lambda e, o=gs, i=gs: e.activation(
                o, i, AF.Exp, bias=LN127, scale=-1.0))
            sch.wait("scalar", [ld, ub_free[pp]])
            su = sch.op("scalar", lambda e, o=ubs[pp], i=rv, s=gs:
                        e.activation(o, i, AF.Identity, bias=0.0, scale=s))
            inring.release(ri, su)
            if RACE_SAFE:
                sm1 = sch.op("scalar", lambda e, o=tms[pp], i=ubs[pp]: e.activation(
                    o, i, AF.Identity, bias=M2, scale=1.0))
                sm2 = sch.op("scalar", lambda e, o=tms[pp], i=tms[pp]: e.activation(
                    o, i, AF.Identity, bias=-M2, scale=1.0))
                sch.wait("vector", su)
                sch.op("vector", lambda e, o=tas[pp], i=ubs[pp]: e.tensor_scalar(
                    o, i, M3, M3, ALU.add, ALU.subtract))
                sch.wait("vector", sm1)
                v4 = sch.op("vector", lambda e, o=ubs[pp], i=ubs[pp]: e.tensor_scalar(
                    o, i, M1, M1, ALU.add, ALU.subtract))
                sch.op("vector", lambda e, o=tas[pp], a=tas[pp], b=ubs[pp]:
                       e.tensor_tensor(o, a, b, ALU.add))
                qi, qv, qf = qb.take()
                sch.wait("vector", [qf, sm2])
                v5 = sch.op("vector", lambda e, o=qv, a=tas[pp], b=tms[pp]:
                            e.tensor_tensor(o, a, b, ALU.subtract))
            else:
                qi, qv, qf = qb.take()
                sch.wait("vector", [qf, su])
                v5 = sch.op("vector", lambda e, o=qv, i=ubs[pp]: e.tensor_scalar(
                    o, i, M1, M1, ALU.add, ALU.subtract))
            ub_free[pp] = v5
            if w2 == 1:
                sch.wait("gpsimd", v5)
                st = sch.dma("gpsimd", d["qwout"][c], qv, stream=f"qst{qi}")
                qb.release(qi, st)
            else:
                last = None
                for hh in range(2):
                    pb = ctx["psum_bf"][:, 4096 * hh: 4096 * hh + 512]
                    ti, tv, tf = tstage.take()
                    sch.wait("tensor", [v5, tf, tbank_free[hh]])
                    tts = None
                    for jj in range(4):
                        ci = 4 * hh + jj
                        tts = sch.op("tensor", lambda e,
                                     o=pb[:, 128 * jj:128 * (jj + 1)],
                                     i=qv[:, 128 * ci:128 * (ci + 1)],
                                     idn=ctx["ident"]: e.transpose(o, i, idn),
                                     inc="t" if jj == 3 else None)
                    sch.wait("scalar", tts)
                    cp = sch.op("scalar", lambda e, o=tv, i=pb: e.copy(o, i))
                    tbank_free[hh] = cp
                    sch.wait("sync", cp)
                    dst3 = d["w1p"][c].rearrange("j p q -> p j q")[:, 4 * hh:4 * hh + 4, :]
                    src3 = tv.rearrange("p (j q) -> p j q", j=4)
                    st = sch.dma("sync", dst3, src3, stream=f"wst{ti}")
                    tstage.release(ti, [st, tts])
                    last = st
                qb.release(qi, last)


def emit_ln(ctx, src, amax_col=None):
    sch, ar, psum = ctx["sch"], ctx["ar"], ctx["psum"]
    onesb, onesrow = ctx["onesb"], ctx["onesrow"]
    A = OFF_A
    TW = 2048
    NG = T // TW
    xgrp = [ar.v(A + c * 8192, TW, F32) for c in range(NC)]
    bfring = Ring([ar.v(A + 65536 + i * 4096, TW, BF16) for i in range(3)])
    sqring = Ring([ar.v(A + 77824 + i * 4096, TW, BF16) for i in range(3)])
    mB = ar.v(A + 90112, TW, F32)
    rB = ar.v(A + 98304, TW, F32)
    tsc = ar.v(A + 106496, TW, F32)
    r_r = ar.v(A + 114688, TW, F32, p0=0, p1=1)
    r_b = ar.v(A + 122880, TW, F32, p0=0, p1=1)
    amax = None if amax_col is None else ctx["vecs"][:, amax_col:amax_col + 1]
    atmp = ctx["vecs"][:, 8:9]
    if amax is not None:
        sch.op("vector", lambda e, ap=amax: e.memset(ap, 0.0))

    xgate = [None] * NC
    evac_done = None
    row_free = None
    prev_apply = None

    for g in range(NG):
        gsl = slice(TW * g, TW * (g + 1))
        stats_last = None
        for c in range(NC):
            sch.wait("sync", xgate[c])
            ld = sch.dma("sync", xgrp[c], src[c][:, gsl], stream=f"lnx{c}")
            bi, bv, bf_ = bfring.take()
            sch.wait("scalar", [bf_, ld])
            cb = sch.op("scalar", lambda e, o=bv, i=xgrp[c]: e.activation(
                o, i, AF.Identity, bias=0.0, scale=1.0))
            si, sv, sf = sqring.take()
            sch.wait("scalar", sf)
            sq = sch.op("scalar", lambda e, o=sv, i=bv: e.activation(
                o, i, AF.Square, bias=0.0, scale=1.0))
            sch.wait("tensor", sq)
            if c == 0:
                sch.wait("tensor", evac_done)
            m2 = None
            for s in range(4):
                ssl = slice(512 * s, 512 * (s + 1))
                m1 = sch.op("tensor", lambda e, o=psum[0:1, ssl], l=onesb[:, 0:1],
                            r=bv[:, ssl], st=(c == 0), sp=(c == NC - 1):
                            e.matmul(o, l, r, start=st, stop=sp),
                            inc="t" if s == 3 else None)
                m2 = sch.op("tensor", lambda e, o=psum[0:1, 2048 + 512 * s:2048 + 512 * (s + 1)],
                            l=onesb[:, 0:1], r=sv[:, ssl],
                            st=(c == 0), sp=(c == NC - 1):
                            e.matmul(o, l, r, start=st, stop=sp),
                            inc="t" if s == 3 else None)
            bfring.release(bi, [m1, sq])
            sqring.release(si, m2)
            stats_last = m2

        sum_row = psum[0:1, 0:2048]
        sq_row = psum[0:1, 2048:4096]
        sch.wait("vector", [stats_last, row_free])
        sch.op("vector", lambda e, o=sum_row, i=sum_row: e.tensor_scalar(
            o, i, 1.0 / D, None, ALU.mult))
        v1 = sch.op("vector", lambda e, o=sq_row, i=sq_row: e.tensor_scalar(
            o, i, 1.0 / D, None, ALU.mult))
        sch.wait("scalar", v1)
        s1 = sch.op("scalar", lambda e, o=r_r, i=sum_row: e.activation(
            o, i, AF.Square, bias=0.0, scale=1.0))
        sch.wait("vector", s1)
        v2 = sch.op("vector", lambda e, o=r_r, a=sq_row, b=r_r: e.tensor_tensor(
            o, a, b, ALU.subtract))
        sch.wait("scalar", v2)
        sch.op("scalar", lambda e, o=r_r, i=r_r: e.activation(
            o, i, AF.Ln, bias=1e-5, scale=1.0))
        s3 = sch.op("scalar", lambda e, o=r_r, i=r_r: e.activation(
            o, i, AF.Exp, bias=0.0, scale=-0.5))
        sch.wait("vector", s3)
        v3 = sch.op("vector", lambda e, o=r_b, a=sum_row, b=r_r: e.tensor_tensor(
            o, a, b, ALU.mult))
        sch.wait("tensor", v3)
        bc_last = None
        for s in range(4):
            sch.op("tensor", lambda e, o=psum[:, 512 * s:512 * (s + 1)],
                   l=onesrow, r=r_b[0:1, 512 * s:512 * (s + 1)]:
                   e.matmul(o, l, r, start=True, stop=True))
            bc_last = sch.op("tensor", lambda e,
                             o=psum[:, 2048 + 512 * s:2048 + 512 * (s + 1)],
                             l=onesrow, r=r_r[0:1, 512 * s:512 * (s + 1)]:
                             e.matmul(o, l, r, start=True, stop=True),
                             inc="t" if s == 3 else None)
        row_free = bc_last
        sch.wait("scalar", [bc_last, prev_apply])
        sch.op("scalar", lambda e, o=mB, i=psum[:, 0:2048]: e.copy(o, i))
        evac_done = sch.op("scalar", lambda e, o=rB, i=psum[:, 2048:4096]:
                           e.copy(o, i))

        va = None
        sch.wait("vector", evac_done)
        for c in range(NC):
            xn_t = ar.v(OFF_B + c * (T * 2) + 2 * TW * g, TW, BF16)
            sch.op("vector", lambda e, o=tsc, a=xgrp[c], b=mB: e.tensor_tensor(
                o, a, b, ALU.subtract))
            va = sch.op("vector", lambda e, o=xn_t, a=tsc, b=rB: e.tensor_tensor(
                o, a, b, ALU.mult))
            if amax is not None:
                sch.op("vector", lambda e, o=atmp, i=xn_t: e.reduce_max(
                    o, i, axis=AX.X, apply_absolute_value=True))
                va = sch.op("vector", lambda e, o=amax, a=amax, b=atmp:
                            e.tensor_tensor(o, a, b, ALU.max))
            xgate[c] = va
        prev_apply = va


def emit_attna(ctx, xres_chunks, xout_chunks):
    sch, ar, psum, d = ctx["sch"], ctx["ar"], ctx["psum"], ctx["d"]
    xn = [ar.v(OFF_B + c * T * 2, T, BF16) for c in range(NC)]
    kr = [ar.v(OFF_A + c * T * 2, T, BF16) for c in range(NC)]
    vtm = [ar.v(OFF_A + 65536 + kb * 2048, D, BF16) for kb in range(NQ)]
    cosb = ar.v(OFF_A + 65536, T, F32)
    sinb = ar.v(OFF_A + 65536 + 16384, T, F32)
    AH = OFF_A + 98304
    wring = Ring([ar.v(AH + i * 256, 128, BF16) for i in range(32)])
    t1 = ar.v(AH + 8192, 512, F32)
    t2 = ar.v(AH + 10240, 512, F32)
    qstage = Ring([ar.v(AH + 12288 + i * 1024, 512, BF16) for i in range(4)])
    S = OFF_S
    wvring = Ring([ar.v(S + i * 2048, D, BF16) for i in range(2)])

    pairs = [(d["wqTt"], True, p) for p in range(4)] + \
            [(d["wkTt"], False, p) for p in range(4)]

    def load_pair(idx):
        wsrc, _, p = pairs[idx]
        toks, views = [], {}
        for half, co in enumerate((p, p + 4)):
            for ci in range(NC):
                wi, wv_, wf = wring.take()
                sch.wait("gpsimd", wf)
                wl = sch.dma("gpsimd", wv_, wsrc[ci, co], stream=f"wq{idx % 2}")
                views[(half, ci)] = (wi, wv_)
                toks.append(wl)
        return toks, views

    prev_ro = None
    banks = [psum[:, 0:512], psum[:, 512:1024],
             psum[:, 1024:1536], psum[:, 1536:2048]]
    bank_free = [None] * 4
    pl = load_pair(0)
    for idx in range(8):
        wsrc, is_q, p = pairs[idx]
        if True:
            wtoks, wviews0 = pl
            wviews = {kk: vv[1] for kk, vv in wviews0.items()}
            sch.wait("sync", prev_ro)
            cl = sch.dma("sync", cosb, d["cosT"][128 * p:128 * (p + 1), :], stream="tab")
            sl = sch.dma("sync", sinb, d["sinT"][128 * p:128 * (p + 1), :], stream="tab")
            if idx < 7:
                pl = load_pair(idx + 1)
            sch.wait_all("tensor", wtoks)
            sch.wait_all("vector", [cl, sl])
            ro = None
            for k in range(NT):
                tsl = slice(512 * k, 512 * (k + 1))
                pe = banks[(k % 2) * 2]
                po = banks[(k % 2) * 2 + 1]
                sch.wait("tensor", bank_free[(k % 2) * 2])
                for ci in range(NC):
                    sch.op("tensor", lambda e, o=pe, l=wviews[(0, ci)],
                           r=xn[ci][:, tsl], st=(ci == 0), sp=(ci == NC - 1):
                           e.matmul(o, l, r, start=st, stop=sp))
                    mmo = sch.op("tensor", lambda e, o=po, l=wviews[(1, ci)],
                                 r=xn[ci][:, tsl], st=(ci == 0), sp=(ci == NC - 1):
                                 e.matmul(o, l, r, start=st, stop=sp),
                                 inc="t" if ci == NC - 1 else None)
                sch.wait("vector", mmo)
                cosv = cosb[:, tsl]
                sinv = sinb[:, tsl]
                if is_q:
                    qi_e, qv_e, qf_e = qstage.take()
                    sch.wait("vector", qf_e)
                    dst_e = qv_e
                    qi_o, qv_o, qf_o = qstage.take()
                    sch.wait("vector", qf_o)
                    dst_o = qv_o
                else:
                    dst_e = kr[p][:, tsl]
                    dst_o = kr[p + 4][:, tsl]
                sch.op("vector", lambda e, o=t1, a=pe, b=cosv: e.tensor_tensor(
                    o, a, b, ALU.mult))
                sch.op("vector", lambda e, o=t2, a=po, b=sinv: e.tensor_tensor(
                    o, a, b, ALU.mult))
                sch.op("vector", lambda e, o=dst_e, a=t1, b=t2: e.tensor_tensor(
                    o, a, b, ALU.add))
                sch.op("vector", lambda e, o=t1, a=pe, b=sinv: e.tensor_tensor(
                    o, a, b, ALU.mult))
                sch.op("vector", lambda e, o=t2, a=po, b=cosv: e.tensor_tensor(
                    o, a, b, ALU.mult))
                ro = sch.op("vector", lambda e, o=dst_o, a=t2, b=t1: e.tensor_tensor(
                    o, a, b, ALU.subtract), inc="v")
                bank_free[(k % 2) * 2] = ro
                bank_free[(k % 2) * 2 + 1] = ro
                if is_q:
                    sch.wait("sync", ro)
                    s1 = sch.dma("sync", d["qr_sp"][p][:, tsl], dst_e, stream=f"qs{qi_e}")
                    s2 = sch.dma("sync", d["qr_sp"][p + 4][:, tsl], dst_o, stream=f"qs{qi_o}")
                    qstage.release(qi_e, s1)
                    qstage.release(qi_o, s2)
            prev_ro = ro
            for (wi, _) in wviews0.values():
                wring.free[wi] = ro

    vbanks = [psum[:, 0:512], psum[:, 512:1024],
              psum[:, 1024:1536], psum[:, 1536:2048]]
    vfree = [prev_ro] * 4
    wvv1 = [ar.v(OFF_A + 65536 + (24 + i) * 2048, D, BF16) for i in range(NC)]
    wvres = [ar.v(OFF_B + ci * T * 2, D, BF16) for ci in range(NC)]
    wtoks = []
    for ci in range(NC):
        wtoks.append(sch.dma("gpsimd", wvv1[ci], d["wvTc"][ci], stream="wvA"))
    sch.wait_all("tensor", wtoks)
    sch.wait_all("vector", list(qstage.free))

    def v_tile(kb, wv_of_ci):
        ksl = slice(128 * kb, 128 * (kb + 1))
        b0 = vbanks[(kb % 2) * 2]
        b1 = vbanks[(kb % 2) * 2 + 1]
        sch.wait("tensor", vfree[(kb % 2) * 2])
        vm = None
        for ci in range(NC):
            wv_ = wv_of_ci(ci)
            sch.op("tensor", lambda e, o=b0, l=xn[ci][:, ksl], r=wv_[:, 0:512],
                   st=(ci == 0), sp=(ci == NC - 1):
                   e.matmul(o, l, r, start=st, stop=sp))
            vm = sch.op("tensor", lambda e, o=b1, l=xn[ci][:, ksl],
                        r=wv_[:, 512:1024], st=(ci == 0), sp=(ci == NC - 1):
                        e.matmul(o, l, r, start=st, stop=sp),
                        inc="t" if ci == NC - 1 else None)
        sch.wait("vector", vm)
        sch.op("vector", lambda e, o=vtm[kb][:, 0:512], i=b0: e.tensor_copy(o, i))
        ve = sch.op("vector", lambda e, o=vtm[kb][:, 512:1024], i=b1:
                    e.tensor_copy(o, i), inc="v")
        vfree[(kb % 2) * 2] = ve
        vfree[(kb % 2) * 2 + 1] = ve
        return vm, ve

    for kb in range(24):
        vm, ve = v_tile(kb, lambda ci: wvv1[ci])
    sch.wait("vector", vm)
    mig = []
    for ci in range(NC):
        mig.append(sch.op("vector", lambda e, o=wvres[ci], i=wvv1[ci]:
                          e.tensor_copy(o, i), inc="v"))
    sch.wait_all("tensor", mig)
    for kb in range(24, NQ):
        vm, ve = v_tile(kb, lambda ci: wvres[ci])

    sch.full_barrier()

    qg = [ar.v(OFF_B + c * 1024, 512, BF16) for c in range(NC)]
    probs = ar.v(OFF_B + 8192, T, BF16)
    PT = ar.v(OFF_B + 16384, NQ * 512, BF16)
    masks = ar.v(OFF_B + 49152, 896, F32)
    xring = Ring([ar.v(OFF_B + 53248 + i * 2048, 512, F32) for i in range(2)])
    oring = Ring([ar.v(OFF_B + 57344 + i * 2048, 512, F32) for i in range(2)])
    mtok = sch.dma("sync", masks, d["maskstrip"], stream="misc")
    sch.wait("vector", mtok)

    ident = ctx["ident"]
    mx = ctx["vecs"][:, 9:10]
    bias = ctx["vecs"][:, 10:11]
    ssum = ctx["vecs"][:, 11:12]
    rsum = ctx["vecs"][:, 12:13]

    bank_free = [None] * 8
    halftog = 0
    qg_free = None
    for g in range(NT):
        qsl = slice(512 * g, 512 * (g + 1))
        sch.wait("sync", qg_free)
        qltoks = []
        for c in range(NC):
            qltoks.append(sch.dma("sync", qg[c], d["qr_sp"][c][:, qsl], stream="qg"))
        sch.wait_all("tensor", qltoks)
        pt_ready = []

        def emit_tp(j, nkb512, pbase, vn):
            tbase = pbase if nkb512 <= 4 else 2048
            tb0 = tbase // 512
            nkb = 4 * nkb512
            sch.wait("tensor", vn)
            for kq in range(0, nkb, 4):
                nn = min(4, nkb - kq)
                tts = None
                for u2 in range(nn):
                    kb = kq + u2
                    sl4 = kb % 4
                    c0 = 2 * (tbase + 512 * sl4)
                    bslot = ctx["psum_bf"][:, c0:c0 + 128]
                    sch.wait("tensor", bank_free[tb0 + sl4])
                    tts = sch.op("tensor", lambda e, o=bslot,
                                 i=probs[:, 128 * kb:128 * (kb + 1)], idn=ident:
                                 e.transpose(o, i, idn),
                                 inc="t" if u2 == nn - 1 else None)
                sch.wait("vector", tts)
                psrc = ctx["psum_bf"][:, 2 * tbase:2 * tbase + 4096]
                psrc3 = psrc.rearrange("p (s r) -> p s r", s=4)[:, 0:nn, 0:128]
                pd = PT[:, 512 * kq:512 * (kq + 4)]
                pd3 = pd.rearrange("p (s r) -> p s r", s=4)[:, 0:nn,
                                                           128 * j:128 * (j + 1)]
                cp = sch.op("vector", lambda e, o=pd3, i=psrc3: e.tensor_copy(o, i))
                for u2 in range(nn):
                    bank_free[tb0 + (kq + u2) % 4] = cp
                pt_ready.append(cp)

        pending_tp = None
        for j in range(4):
            qi = 4 * g + j
            nkb512 = qi // 4 + 1
            Lb = 512 * nkb512
            if nkb512 <= 4:
                hh = halftog
                halftog ^= 1
                pbase = 2048 * hh
            else:
                pbase = 0
            b0 = pbase // 512
            mm = None
            flush_after_first = pending_tp
            for kb5 in range(nkb512):
                sch.wait("tensor", bank_free[b0 + kb5])
                for ci in range(NC):
                    mm = sch.op("tensor", lambda e,
                                o=psum[:, pbase + 512 * kb5:pbase + 512 * (kb5 + 1)],
                                l=qg[ci][:, 128 * j:128 * (j + 1)],
                                r=kr[ci][:, 512 * kb5:512 * (kb5 + 1)],
                                st=(ci == 0), sp=(ci == NC - 1):
                                e.matmul(o, l, r, start=st, stop=sp),
                                inc="t" if ci == NC - 1 else None)
                if kb5 == 0 and flush_after_first is not None:
                    emit_tp(*flush_after_first)
                    flush_after_first = None
                    pending_tp = None
                if kb5 < nkb512 - 1:
                    sch.wait("vector", mm)
                    if kb5 == 0:
                        sch.op("vector", lambda e, o=mx,
                               i=psum[:, pbase + 512 * kb5:pbase + 512 * (kb5 + 1)]:
                               e.reduce_max(o, i, axis=AX.X))
                    else:
                        sch.op("vector", lambda e, o=bias,
                               i=psum[:, pbase + 512 * kb5:pbase + 512 * (kb5 + 1)]:
                               e.reduce_max(o, i, axis=AX.X))
                        sch.op("vector", lambda e, o=mx, a=mx, b=bias:
                               e.tensor_tensor(o, a, b, ALU.max))
            sch.wait("vector", mm)
            moff = 384 - 128 * (qi % 4)
            dsl = slice(pbase + 512 * (nkb512 - 1), pbase + Lb)
            sch.op("vector", lambda e, o=psum[:, dsl], a=psum[:, dsl],
                   b=masks[:, moff:moff + 512]: e.tensor_tensor(o, a, b, ALU.min))
            sch.op("vector", lambda e, o=bias, i=psum[:, dsl]: e.reduce_max(
                o, i, axis=AX.X))
            if nkb512 > 1:
                sch.op("vector", lambda e, o=mx, a=mx, b=bias: e.tensor_tensor(
                    o, a, b, ALU.max))
            else:
                sch.op("vector", lambda e, o=mx, i=bias: e.tensor_copy(o, i))
            vb = sch.op("vector", lambda e, o=bias, i=mx: e.tensor_scalar(
                o, i, -1.0 / 32.0, None, ALU.mult))
            sch.wait("scalar", vb)
            se = sch.op("scalar", lambda e, o=probs[:, 0:Lb],
                        i=psum[:, pbase:pbase + Lb],
                        b2=bias: e.activation(o, i, AF.Exp, bias=b2, scale=1.0 / 32.0,
                                              accum_out=ssum))
            for bb_ in range(nkb512):
                bank_free[b0 + bb_] = se
            sch.op("scalar", lambda e, o=rsum, i=ssum: e.activation(
                o, i, AF.Ln, bias=0.0, scale=1.0))
            s6 = sch.op("scalar", lambda e, o=rsum, i=rsum: e.activation(
                o, i, AF.Exp, bias=0.0, scale=-1.0))
            sch.wait("vector", [se, s6])
            vn = sch.op("vector", lambda e, o=probs[:, 0:Lb], i=probs[:, 0:Lb],
                        s=rsum: e.tensor_scalar(o, i, s, None, ALU.mult))
            pending_tp = (j, nkb512, pbase, vn)
        if pending_tp is not None:
            emit_tp(*pending_tp)
            pending_tp = None
        sch.wait_all("tensor", pt_ready)
        nkb = 4 * (g + 1)
        va = None
        for dc in range(NC):
            sch.wait("tensor", bank_free[dc])
            mmv = None
            for kb in range(nkb):
                mmv = sch.op("tensor", lambda e, o=psum[:, 512 * dc:512 * (dc + 1)],
                             l=vtm[kb][:, 128 * dc:128 * (dc + 1)],
                             r=PT[:, 512 * kb:512 * (kb + 1)],
                             st=(kb == 0), sp=(kb == nkb - 1):
                             e.matmul(o, l, r, start=st, stop=sp),
                             inc="t" if kb == nkb - 1 else None)
            ri, xv, xf = xring.take()
            sch.wait("sync", xf)
            ld = sch.dma("sync", xv, xres_chunks[dc][:, qsl], stream=f"ax{ri}")
            oi, ov, of = oring.take()
            sch.wait("vector", [of, ld, mmv])
            va = sch.op("vector", lambda e, o=ov, a=xv,
                        b=psum[:, 512 * dc:512 * (dc + 1)]:
                        e.tensor_tensor(o, a, b, ALU.add))
            xring.release(ri, va)
            bank_free[dc] = va
            sch.wait("sync", va)
            st = sch.dma("sync", xout_chunks[dc][:, qsl], ov, stream=f"ao{oi}")
            oring.release(oi, st)
            if "xst_tok" in ctx:
                ctx["xst_tok"][dc][g] = st
        qg_free = va


def _bcast_scalar(ctx, src11, dst_col, pre_tok):
    sch, psum = ctx["sch"], ctx["psum"]
    dst = ctx["vecs"][:, dst_col:dst_col + 1]
    pb = psum[:, 3584:3592]
    sch.wait("tensor", pre_tok)
    tm = sch.op("tensor", lambda e, o=pb[:, 0:1], l=ctx["onesrow"], r=src11:
                e.matmul(o, l, r, start=True, stop=True), inc="t")
    sch.wait("vector", tm)
    vt = sch.op("vector", lambda e, o=dst, i=pb[:, 0:1]: e.tensor_copy(o, i), inc="v")
    return dst, vt


def _amax_allreduce(ctx, amax_vec, cc_idx, pre_toks):
    sch, ar, psum, d = ctx["sch"], ctx["ar"], ctx["psum"], ctx["d"]
    cc_in, cc_out = d["cc"][cc_idx]
    r8 = ar.v(OFF_S + 4096, 8, F32, p0=0, p1=1)
    g11 = ar.v(OFF_S + 4128, 1, F32, p0=0, p1=1)
    ab = ar.v(OFF_S + 4352, 1, BF16)
    pb = ctx["psum_bf"][0:1, 2 * 3592:2 * 3592 + 128]
    sch.wait_all("vector", pre_toks)
    cb = sch.op("vector", lambda e, o=ab, i=amax_vec: e.tensor_copy(o, i), inc="v")
    sch.wait("tensor", cb)
    tt = sch.op("tensor", lambda e, o=pb, i=ab, idn=ctx["ident"]:
                e.transpose(o, i, idn), inc="t")
    sch.wait("vector", tt)
    sch.op("vector", lambda e, o=r8[0:1, 0:1], i=pb: e.reduce_max(o, i, axis=AX.X))
    v1 = sch.op("vector", lambda e, o=r8[0:1, 0:1], i=r8[0:1, 0:1]:
                e.tensor_scalar(o, i, 1e-8, None, ALU.max), inc="v")
    sch.wait("vector", v1)
    v2 = sch.op("vector", lambda e, o=r8, i=r8[0:1, 0:1].to_broadcast((1, 8)):
                e.tensor_copy(o, i), inc="v")
    sch.wait("gpsimd", v2)
    d1 = sch.dma("gpsimd", cc_in, r8, stream="ccd")
    sch.wait("gpsimd", d1)
    ranks = CC_RANKS if CC_RANKS is not None else list(range(B))
    cc = sch.op("gpsimd", lambda e, i=cc_in, o=cc_out, rg=[ranks]: e.collective_compute(
        "AllReduce", ALU.max, replica_groups=rg,
        ins=[i], outs=[o]), inc="cc")
    sch.wait("gpsimd", cc)
    d2 = sch.dma("gpsimd", g11, cc_out[0:1, 0:1], stream="ccd")
    return g11, d2


def emit_ffn(ctx, app, xsrc, xdst):
    sch, ar, psum, d = ctx["sch"], ctx["ar"], ctx["psum"], ctx["d"]
    A = OFF_A
    xn = [ar.v(OFF_B + c * T * 2, T, BF16) for c in range(NC)]
    amax_x = ctx["vecs"][:, 0:1]

    g11, gtok = _amax_allreduce(ctx, amax_x, app * 2, [])
    sc11 = ar.v(OFF_S + 4160, 1, F32, p0=0, p1=1)
    iv11 = ar.v(OFF_S + 4192, 1, F32, p0=0, p1=1)
    sch.wait("scalar", gtok)
    s0 = sch.op("scalar", lambda e, o=sc11, i=g11: e.activation(
        o, i, AF.Ln, bias=0.0, scale=1.0))
    s1 = sch.op("scalar", lambda e, o=sc11, i=sc11: e.activation(
        o, i, AF.Exp, bias=0.0, scale=-1.0))
    sch.wait("vector", [s1, gtok])
    v2 = sch.op("vector", lambda e, o=sc11, i=sc11: e.tensor_scalar(
        o, i, 32767.0, None, ALU.mult))
    v3 = sch.op("vector", lambda e, o=iv11, i=g11: e.tensor_scalar(
        o, i, 1.0 / 32767.0, None, ALU.mult))
    scale_vec, st1 = _bcast_scalar(ctx, sc11, 1, v2)
    inv_vec, st2 = _bcast_scalar(ctx, iv11, 2, [v3, st1])

    ubs = [ar.v(A + 0 + i * 16384, T, F32) for i in range(2)]
    tas = [ar.v(A + 32768 + i * 16384, T, F32) for i in range(2)]
    tms = [ar.v(A + 65536 + i * 16384, T, F32) for i in range(2)]
    sch.wait_all("scalar", [st1, st2, v3])
    prev_q = [None, None]
    for c in range(NC):
        pp = c % 2
        tile = ar.v(OFF_B + c * T * 2, T, BF16)
        sch.wait("scalar", prev_q[pp])
        s1 = sch.op("scalar", lambda e, o=ubs[pp], i=tile, s=scale_vec: e.activation(
            o, i, AF.Identity, bias=0.0, scale=s))
        if RACE_SAFE:
            sm1 = sch.op("scalar", lambda e, o=tms[pp], i=ubs[pp]: e.activation(o, i, AF.Identity, bias=M2, scale=1.0))
            sm2 = sch.op("scalar", lambda e, o=tms[pp], i=tms[pp]: e.activation(o, i, AF.Identity, bias=-M2, scale=1.0))
            sch.wait("vector", s1)
            sch.op("vector", lambda e, o=tas[pp], i=ubs[pp]: e.tensor_scalar(
                o, i, M3, M3, ALU.add, ALU.subtract))
            sch.wait("vector", sm1)
            v4 = sch.op("vector", lambda e, o=ubs[pp], i=ubs[pp]: e.tensor_scalar(
                o, i, M1, M1, ALU.add, ALU.subtract))
            sch.op("vector", lambda e, o=tas[pp], a=tas[pp], b=ubs[pp]: e.tensor_tensor(
                o, a, b, ALU.add))
            sch.wait("vector", sm2)
            pq = sch.op("vector", lambda e, o=tile, a=tas[pp], b=tms[pp]: e.tensor_tensor(
                o, a, b, ALU.subtract))
        else:
            sch.wait("vector", s1)
            pq = sch.op("vector", lambda e, o=tile, i=ubs[pp]: e.tensor_scalar(
                o, i, M1, M1, ALU.add, ALU.subtract))
        prev_q[pp] = pq
    prev_q = [prev_q[0], prev_q[1]]

    svW1 = ar.v(A + 98304, FC, F32)
    sch.wait("vector", prev_q)
    sv_tok = None
    for Fc in range(FC):
        sv_tok = sch.op("vector", lambda e, o=svW1[:, Fc:Fc + 1],
                        a=ctx["invw"][:, Fc:Fc + 1], b=inv_vec:
                        e.tensor_tensor(o, a, b, ALU.mult),
                        inc="v" if Fc == FC - 1 else None)

    w1ring = Ring([ar.v(A + 100352 + i * 256, 128, BF16) for i in range(16)])
    hstage = Ring([ar.v(A + 104448 + i * 1024, 512, BF16) for i in range(4)])
    amax_h = ctx["vecs"][:, 3:4]
    ah_tmp = ctx["vecs"][:, 8:9]
    sch.op("vector", lambda e, ap=amax_h: e.memset(ap, 0.0), inc="v")
    sch.wait_all("tensor", [prev_q, sv_tok])
    hbanks = [psum[:, 512 * i:512 * (i + 1)] for i in range(4)]
    hfree = [None] * 4
    amtok = None
    for Fc in range(FC):
        wtoks = []
        wviews = []
        for ci in range(NC):
            wi, wv_, wf = w1ring.take()
            sch.wait("gpsimd", wf)
            wl = sch.dma("gpsimd", wv_, d["w1p"][Fc, ci], stream=f"w1{Fc % 2}")
            wviews.append((wi, wv_))
            wtoks.append(wl)
        sch.wait_all("tensor", wtoks)
        lastmm = None
        for k in range(NT):
            tsl = slice(512 * k, 512 * (k + 1))
            bk = hbanks[k % 4]
            sch.wait("tensor", hfree[k % 4])
            for ci in range(NC):
                lastmm = sch.op("tensor", lambda e, o=bk, l=wviews[ci][1],
                                r=xn[ci][:, tsl], st=(ci == 0), sp=(ci == NC - 1):
                                e.matmul(o, l, r, start=st, stop=sp),
                                inc="t" if ci == NC - 1 else None)
            hi, hv, hf = hstage.take()
            sch.wait("scalar", hf)
            sch.wait("scalar", lastmm)
            ge = sch.op("scalar", lambda e, o=hv, i=bk, s=svW1[:, Fc:Fc + 1],
                        b2=ctx["binsb"][:, Fc:Fc + 1]:
                        e.activation(o, i, AF.Gelu, bias=b2, scale=s), inc="s")
            hfree[k % 4] = ge
            sch.wait("vector", ge)
            sch.op("vector", lambda e, o=ah_tmp, i=hv: e.reduce_max(
                o, i, axis=AX.X, apply_absolute_value=True))
            amtok = sch.op("vector", lambda e, o=amax_h, a=amax_h, b=ah_tmp:
                           e.tensor_tensor(o, a, b, ALU.max), inc="v")
            sch.wait("sync", ge)
            hs = sch.dma("sync", d["hid"][Fc, k], hv, stream=f"hs{hi}")
            hstage.release(hi, [amtok, hs])
        for wi, _ in wviews:
            w1ring.free[wi] = lastmm

    g2, gtok2 = _amax_allreduce(ctx, amax_h, app * 2 + 1, [amtok])
    sc2 = ar.v(OFF_S + 4224, 1, F32, p0=0, p1=1)
    iv2 = ar.v(OFF_S + 4256, 1, F32, p0=0, p1=1)
    sch.wait("scalar", gtok2)
    s0b = sch.op("scalar", lambda e, o=sc2, i=g2: e.activation(
        o, i, AF.Ln, bias=0.0, scale=1.0))
    s1b = sch.op("scalar", lambda e, o=sc2, i=sc2: e.activation(
        o, i, AF.Exp, bias=0.0, scale=-1.0))
    sch.wait("vector", [s1b, gtok2])
    v2 = sch.op("vector", lambda e, o=sc2, i=sc2: e.tensor_scalar(
        o, i, 32767.0, None, ALU.mult))
    v3 = sch.op("vector", lambda e, o=iv2, i=g2: e.tensor_scalar(
        o, i, 1.0 / 32767.0, None, ALU.mult))
    scale_h, sh1 = _bcast_scalar(ctx, sc2, 4, v2)
    inv_h, sh2 = _bcast_scalar(ctx, iv2, 5, [v3, sh1])

    w2 = [ar.v(OFF_B + c * 2048, D, BF16) for c in range(FC)]
    vecs2 = ar.v(A + 4096, FC, F32)
    qtoks = []
    sch.wait("gpsimd", lastmm)
    for c in range(FC):
        qtoks.append(sch.dma("gpsimd", w2[c], d["qwout"][c],
                             stream=f"qw{c % 2}"))
    sch.wait_all("vector", [sh1, sh2, v3])
    vc_tok = None
    for c in range(FC):
        vc_tok = sch.op("vector", lambda e, o=vecs2[:, c:c + 1],
                        a=ctx["invw"][:, FC + c:FC + c + 1], b=inv_h:
                        e.tensor_tensor(o, a, b, ALU.mult),
                        inc="v" if c == FC - 1 else None)
    w2toks = []
    sch.wait_all("vector", qtoks)
    for c in range(FC):
        wt = sch.op("vector", lambda e, o=w2[c], i=w2[c], s=vecs2[:, c:c + 1]:
                    e.tensor_scalar(o, i, s, None, ALU.mult), inc="v")
        w2toks.append(wt)

    NDEEP = 3
    hbuf = Ring([ar.v(A + 0 + i * 4096, 2048, BF16) for i in range(3)])
    hbuf.free = [w2toks, w2toks, w2toks]
    ub2 = [ar.v(A + 12288 + i * 8192, 2048, F32) for i in range(NDEEP)]
    t64b = [ar.v(A + 36864 + i * 8192, 2048, F32) for i in range(NDEEP)]
    tab = [ar.v(A + 61440 + i * 8192, 2048, F32) for i in range(NDEEP)]
    qhk = [ar.v(A + 86016 + g * 4096, 2048, BF16) for g in range(8)]
    xring = Ring([ar.v(A + 118784 + i * 2048, 512, F32) for i in range(4)])
    orng = Ring([ar.v(A + 126976 + i * 2048, 512, F32) for i in range(2)])
    sch.wait_all("scalar", w2toks)
    sch.wait_all("tensor", w2toks)
    sch.wait_all("sync", list(hstage.free))
    ub_free2 = [None] * NDEEP
    qh_free = [None] * 8
    half_free = [None, None]
    vq_tok = [None] * 8
    pend = None
    for k in range(NT):
        tsl = slice(512 * k, 512 * (k + 1))

        def requant(cg):
            pp = cg % NDEEP
            hi, hv, hf = hbuf.take()
            sch.wait("sync", hf)
            ldt = []
            for j in range(4):
                ldt.append(sch.dma("sync", hv[:, 512 * j:512 * (j + 1)],
                                   d["hid"][4 * cg + j, k], stream=f"hr{hi}"))
            sch.wait("scalar", ldt)
            sch.wait("scalar", ub_free2[pp])
            s1 = sch.op("scalar", lambda e, o=ub2[pp], i=hv, s=scale_h: e.activation(
                o, i, AF.Identity, bias=0.0, scale=s))
            hbuf.release(hi, s1)
            if RACE_SAFE:
                s2 = sch.op("scalar", lambda e, o=t64b[pp], i=ub2[pp]: e.activation(
                    o, i, AF.Identity, bias=M3, scale=1.0))
                s3 = sch.op("scalar", lambda e, o=t64b[pp], i=t64b[pp]: e.activation(
                    o, i, AF.Identity, bias=-M3, scale=1.0))
                sch.wait("vector", s1)
                sch.op("vector", lambda e, o=tab[pp], i=ub2[pp]: e.tensor_scalar(
                    o, i, M1, M1, ALU.add, ALU.subtract))
                sch.wait("vector", s2)
                v4 = sch.op("vector", lambda e, o=ub2[pp], i=ub2[pp]: e.tensor_scalar(
                    o, i, M2, M2, ALU.add, ALU.subtract))
                sch.wait("vector", s3)
                sch.op("vector", lambda e, o=tab[pp], a=tab[pp], b=t64b[pp]:
                       e.tensor_tensor(o, a, b, ALU.add))
                sch.wait("vector", qh_free[cg])
                vq = sch.op("vector", lambda e, o=qhk[cg], a=tab[pp], b=ub2[pp]:
                            e.tensor_tensor(o, a, b, ALU.subtract))
            else:
                sch.wait("vector", [s1, qh_free[cg]])
                vq = sch.op("vector", lambda e, o=qhk[cg], i=ub2[pp]: e.tensor_scalar(
                    o, i, M1, M1, ALU.add, ALU.subtract))
            ub_free2[pp] = vq
            vq_tok[cg] = vq

        def emit_evac(half, ptsl, mm_tok):
            va = None
            for dd2 in range(4):
                dc = 4 * half + dd2
                ri, xv, xf = xring.take()
                sch.wait("sync", xf)
                ld = sch.dma("sync", xv, xsrc[dc][:, ptsl], stream=f"fx{ri}")
                oi, ov, of = orng.take()
                sch.wait("vector", [of, ld, mm_tok])
                va = sch.op("vector", lambda e, o=ov,
                            i0=psum[:, 512 * dc:512 * (dc + 1)],
                            s=ctx["bout2sb"][:, dc:dc + 1], i1=xv:
                            e.scalar_tensor_tensor(o, i0, s, i1, ALU.add, ALU.add))
                xring.release(ri, va)
                sch.wait("sync", va)
                st = sch.dma("sync", xdst[dc][:, ptsl], ov, stream=f"fo{oi}")
                orng.release(oi, st)
            half_free[half] = va

        if pend is not None:
            emit_evac(0, pend[0], pend[1])
        sch.wait("tensor", half_free[0])
        mmh = None
        for cg in range(8):
            requant(cg)
            sch.wait("tensor", vq_tok[cg])
            for j in range(4):
                c = 4 * cg + j
                for dd2 in range(4):
                    dc = dd2
                    mmh = sch.op("tensor", lambda e,
                                 o=psum[:, 512 * dc:512 * (dc + 1)],
                                 l=w2[c][:, 128 * dc:128 * (dc + 1)],
                                 r=qhk[cg][:, 512 * j:512 * (j + 1)],
                                 st=(c == 0), sp=(c == FC - 1):
                                 e.matmul(o, l, r, start=st, stop=sp),
                                 inc="t" if (c == FC - 1 and dd2 == 3) else None)
            if cg == 1 and pend is not None:
                emit_evac(1, pend[0], pend[2])
                pend = None
        mm_half = [mmh, None]
        sch.wait("tensor", half_free[1])
        for cg in range(8):
            sch.wait("tensor", vq_tok[cg])
            mmc = None
            for j in range(4):
                c = 4 * cg + j
                for dd2 in range(4):
                    dc = 4 + dd2
                    mmc = sch.op("tensor", lambda e,
                                 o=psum[:, 512 * dc:512 * (dc + 1)],
                                 l=w2[c][:, 128 * dc:128 * (dc + 1)],
                                 r=qhk[cg][:, 512 * j:512 * (j + 1)],
                                 st=(c == 0), sp=(c == FC - 1):
                                 e.matmul(o, l, r, start=st, stop=sp),
                                 inc="t" if (j == 3 and dd2 == 3) else None)
            qh_free[cg] = mmc
        mm_half[1] = mmc
        pend = (tsl, mm_half[0], mm_half[1])
    emit_evac(0, pend[0], pend[1])
    emit_evac(1, pend[0], pend[2])


def emit_attnb(ctx, xsrc, xdst):
    sch, ar, psum, d = ctx["sch"], ctx["ar"], ctx["psum"], ctx["d"]
    xn = [ar.v(OFF_B + c * T * 2, T, BF16) for c in range(NC)]
    qphi = [ar.v(OFF_A + c * T * 2, T, BF16) for c in range(NC)]
    AH = OFF_A + 65536
    mixer = [ar.v(AH + c * T * 2, T, BF16) for c in range(NC)]
    onesb = ctx["onesb"]

    wring = Ring([ar.v(AH + i * 256, 128, BF16) for i in range(16)])
    s1f = ar.v(AH + 4096, 512, F32)
    s2f = ar.v(AH + 6144, 512, F32)
    qbanks = [psum[:, 512 * i:512 * (i + 1)] for i in range(4)]
    qfree = [None] * 4
    for co in range(NC):
        wtoks = []
        wviews = []
        for ci in range(NC):
            wi, wv_, wf = wring.take()
            sch.wait("gpsimd", wf)
            wl = sch.dma("gpsimd", wv_, d["wqkvTqt"][ci, co], stream=f"bw{co % 2}")
            wviews.append((wi, wv_))
            wtoks.append(wl)
        sch.wait_all("tensor", wtoks)
        lastmm = None
        for k in range(NT):
            tsl = slice(512 * k, 512 * (k + 1))
            bk = qbanks[k % 4]
            sch.wait("tensor", qfree[k % 4])
            for ci in range(NC):
                lastmm = sch.op("tensor", lambda e, o=bk, l=wviews[ci][1],
                                r=xn[ci][:, tsl], st=(ci == 0), sp=(ci == NC - 1):
                                e.matmul(o, l, r, start=st, stop=sp),
                                inc="t" if ci == NC - 1 else None)
            sch.wait("vector", lastmm)
            v1 = sch.op("vector", lambda e, o=s1f, i=bk: e.tensor_scalar(
                o, i, 0.0, None, ALU.min), inc="v")
            sch.wait("scalar", v1)
            se = sch.op("scalar", lambda e, o=s2f, i=s1f: e.activation(
                o, i, AF.Exp, bias=0.0, scale=1.0), inc="s")
            sch.wait("vector", se)
            vphi = sch.op("vector", lambda e, o=qphi[co][:, tsl], i0=bk,
                          i1=s2f: e.scalar_tensor_tensor(
                              o, i0, 0.0, i1, ALU.max, ALU.add), inc="v")
            qfree[k % 4] = vphi
        for wi, _ in wviews:
            wring.free[wi] = lastmm

    wkv = [ar.v(AH + 8192 + ci * 4096, 2 * D, BF16) for ci in range(NC)]
    kvtoks = []
    for ci in range(NC):
        kvtoks.append(sch.dma("gpsimd", wkv[ci], d["wqkvTkv"][ci], stream="bkv"))
    sch.wait_all("tensor", kvtoks)
    kf1 = ar.v(AH + 40960, D, F32)
    kf2 = ar.v(AH + 45056, D, F32)
    kphi_t = [ar.v(AH + 49152, D, BF16), ar.v(AH + 57344, D, BF16)]
    vtile = [ar.v(AH + 51200, D, BF16), ar.v(AH + 59392, D, BF16)]
    ksrow = ar.v(AH + 53248, D, F32, p0=0, p1=1)
    kps = psum[:, 0:1024]
    vps = psum[:, 1024:2048]
    kvps = psum[:, 2048:3072]
    ksps = psum[0:1, 3072:4096]
    evac_tok = list(qfree)
    kv_last = None
    ks_last = None
    phi_free = [None, None]
    for kb in range(NQ):
        ksl = slice(128 * kb, 128 * (kb + 1))
        sl2 = kb % 2
        sch.wait("tensor", evac_tok)
        kmm = None
        for piece in range(2):
            for ci in range(NC):
                kmm = sch.op("tensor", lambda e,
                             o=psum[:, 512 * piece:512 * (piece + 1)],
                             l=xn[ci][:, ksl],
                             r=wkv[ci][:, 512 * piece:512 * (piece + 1)],
                             st=(ci == 0), sp=(ci == NC - 1):
                             e.matmul(o, l, r, start=st, stop=sp),
                             inc="t" if (ci == NC - 1 and piece == 1) else None)
        vmm = None
        for piece in range(2, 4):
            for ci in range(NC):
                vmm = sch.op("tensor", lambda e,
                             o=psum[:, 512 * piece:512 * (piece + 1)],
                             l=xn[ci][:, ksl],
                             r=wkv[ci][:, 512 * piece:512 * (piece + 1)],
                             st=(ci == 0), sp=(ci == NC - 1):
                             e.matmul(o, l, r, start=st, stop=sp),
                             inc="t" if (ci == NC - 1 and piece == 3) else None)
        sch.wait("vector", kmm)
        v1 = sch.op("vector", lambda e, o=kf1, i=kps: e.tensor_scalar(
            o, i, 0.0, None, ALU.min), inc="v")
        sch.wait("scalar", v1)
        se = sch.op("scalar", lambda e, o=kf2, i=kf1: e.activation(
            o, i, AF.Exp, bias=0.0, scale=1.0), inc="s")
        sch.wait("vector", [se, phi_free[sl2]])
        vphi = sch.op("vector", lambda e, o=kphi_t[sl2], i0=kps, i1=kf2:
                      e.scalar_tensor_tensor(o, i0, 0.0, i1, ALU.max, ALU.add),
                      inc="v")
        sch.wait("vector", vmm)
        ev = sch.op("vector", lambda e, o=vtile[sl2], i=vps: e.tensor_copy(o, i),
                    inc="v")
        evac_tok = [vphi, ev]
        sch.wait("tensor", ev)
        for c in range(NC):
            kv_last = sch.op("tensor", lambda e,
                             o=kvps[:, 128 * c:128 * (c + 1)],
                             l=kphi_t[sl2][:, 128 * c:128 * (c + 1)],
                             r=vtile[sl2][:, 128 * c:128 * (c + 1)],
                             st=(kb == 0 and c % 4 == 0),
                             sp=(kb == NQ - 1 and c % 4 == 3):
                             e.matmul(o, l, r, start=st, stop=sp),
                             inc="t" if c == NC - 1 else None)
        for hh in range(2):
            ks_last = sch.op("tensor", lambda e,
                             o=ksps[:, 512 * hh:512 * (hh + 1)],
                             l=onesb[:, 0:1],
                             r=kphi_t[sl2][:, 512 * hh:512 * (hh + 1)],
                             st=(kb == 0), sp=(kb == NQ - 1):
                             e.matmul(o, l, r, start=st, stop=sp),
                             inc="t" if hh == 1 else None)
        phi_free[sl2] = [kv_last, ks_last]

    wout_sb = [ar.v(OFF_B + 24576 + ci * 2048, D, BF16) for ci in range(NC)]
    wotoks = []
    sch.wait("gpsimd", [kv_last, ks_last])
    for ci in range(NC):
        wotoks.append(sch.dma("gpsimd", wout_sb[ci], d["woutTc"][ci], stream="bwo"))

    sch.wait("vector", ks_last)
    vk = sch.op("vector", lambda e, o=ksrow, i=ksps: e.tensor_copy(o, i), inc="v")
    ksb = ctx["vecs"][:, 20:28]
    sch.wait("tensor", vk)
    ttoks = []
    for s in range(NC):
        tt = sch.op("tensor", lambda e, o=psum[:, 3072 + s:3072 + s + 1],
                    i=ksrow[0:1, 128 * s:128 * (s + 1)],
                    idn=ctx["onesrow"][0:1, 0:1]:
                    e.transpose(o, i, idn), inc="t")
        ttoks.append(tt)
    sch.wait_all("vector", ttoks)
    vk2 = sch.op("vector", lambda e, o=ksb, i=psum[:, 3072:3080]:
                 e.tensor_copy(o, i), inc="v")
    sch.wait("vector", vk2)
    blk = ctx["blk"]
    btoks = []
    for c in range(NC):
        sch.op("vector", lambda e, o=blk[0:64, 16 * c + 2 * c:16 * c + 2 * c + 1],
               i=ksb[0:64, c:c + 1]: e.tensor_copy(o, i))
        bt = sch.op("vector", lambda e,
                    o=blk[64:128, 16 * c + 2 * c + 1:16 * c + 2 * c + 2],
                    i=ksb[64:128, c:c + 1]: e.tensor_copy(o, i), inc="v")
        btoks.append(bt)

    zden = ar.v(OFF_B + 0, T, F32, p0=0, p1=16)
    zbank2 = [psum[0:16, 0:512], psum[0:16, 512:1024]]
    sch.wait_all("tensor", btoks)
    zev = [None, None]
    for k in range(NT):
        tsl = slice(512 * k, 512 * (k + 1))
        zi = k % 2
        sch.wait("tensor", zev[zi])
        lastz = None
        for c in range(NC):
            lastz = sch.op("tensor", lambda e, o=zbank2[zi],
                           l=blk[:, 16 * c:16 * (c + 1)], r=qphi[c][:, tsl],
                           st=(c == 0), sp=(c == NC - 1):
                           e.matmul(o, l, r, start=st, stop=sp),
                           inc="t" if c == NC - 1 else None)
        sch.wait("vector", lastz)
        zev[zi] = sch.op("vector", lambda e, o=zden[0:16, tsl], i=zbank2[zi]:
                         e.tensor_copy(o, i), inc="v")
    v5 = sch.op("vector", lambda e, o=zden, i=zden: e.tensor_scalar(
        o, i, 1e-6, None, ALU.add), inc="v")
    sch.wait("scalar", v5)
    sch.op("scalar", lambda e, o=zden, i=zden: e.activation(
        o, i, AF.Ln, bias=0.0, scale=1.0))
    v6 = sch.op("scalar", lambda e, o=zden, i=zden: e.activation(
        o, i, AF.Exp, bias=0.0, scale=-1.0), inc="s")

    sch.wait("vector", kv_last)
    kvt = sch.op("vector", lambda e, o=ctx["kvsb"], i=kvps: e.tensor_copy(o, i),
                 inc="v")

    zbankA = [psum[:, 0:512], psum[:, 512:1024]]
    zbankB = [psum[:, 1024:1536], psum[:, 1536:2048]]
    zbs = [ar.v(OFF_B + 20480 + i * 2048, 512, F32) for i in range(2)]
    zfree = [None, None]
    zsfree = [None, None]
    sch.wait_all("tensor", [v6, kvt])
    for c in range(NC):
        for k in range(NT):
            tsl = slice(512 * k, 512 * (k + 1))
            i2 = (c * NT + k) % 2
            ba = zbankA[i2]
            bb2 = zbankB[i2]
            sch.wait("tensor", zfree[i2])
            zm = sch.op("tensor", lambda e, o=ba, l=ctx["indsb"][:, 128 * c:128 * (c + 1)],
                        r=zden[0:16, tsl]: e.matmul(o, l, r, start=True, stop=True),
                        inc="t")
            sch.op("tensor", lambda e, o=bb2[0:64, :],
                   l=ctx["kvsb"][0:64, 128 * c:128 * c + 64],
                   r=qphi[c][0:64, tsl]:
                   e.matmul(o, l, r, start=True, stop=True))
            mm2 = sch.op("tensor", lambda e, o=bb2[64:128, :],
                         l=ctx["kvsb"][64:128, 128 * c + 64:128 * c + 128],
                         r=qphi[c][64:128, tsl], tp=(64, 64):
                         e.matmul(o, l, r, start=True, stop=True, tile_position=tp),
                         inc="t")
            sch.wait("scalar", zm)
            sch.wait("scalar", zsfree[i2])
            zcp = sch.op("scalar", lambda e, o=zbs[i2], i=ba: e.copy(o, i), inc="s")
            sch.wait("vector", mm2)
            sch.wait("vector", zcp)
            vm = sch.op("vector", lambda e, o=mixer[c][:, tsl], a=bb2, b=zbs[i2]:
                        e.tensor_tensor(o, a, b, ALU.mult), inc="v")
            zfree[i2] = [vm, zcp]
            zsfree[i2] = vm

    sch.wait_all("tensor", wotoks)
    xring = Ring([ar.v(OFF_B + 45056 + i * 2048, 512, F32) for i in range(2)])
    orng = Ring([ar.v(OFF_B + 49152 + i * 2048, 512, F32) for i in range(2)])
    xring.free = [vm, vm]
    orng.free = [vm, vm]
    evac_done = [[zfree[0], zfree[1]], None]
    for k in range(NT):
        tsl = slice(512 * k, 512 * (k + 1))
        for half in range(2):
            sch.wait("tensor", evac_done[half])
            mmt = None
            for ci in range(NC):
                for co2 in range(4):
                    co = 4 * half + co2
                    mmt = sch.op("tensor", lambda e,
                                 o=psum[:, 512 * co:512 * (co + 1)],
                                 l=wout_sb[ci][:, 128 * co:128 * (co + 1)],
                                 r=mixer[ci][:, tsl], st=(ci == 0),
                                 sp=(ci == NC - 1):
                                 e.matmul(o, l, r, start=st, stop=sp),
                                 inc="t" if (ci == NC - 1 and co2 == 3) else None)
            va = None
            for co2 in range(4):
                co = 4 * half + co2
                ri, xv, xf = xring.take()
                sch.wait("sync", xf)
                ld = sch.dma("sync", xv, xsrc[co][:, tsl], stream=f"bx{ri}")
                oi, ov, of = orng.take()
                sch.wait("vector", [of, ld, mmt])
                va = sch.op("vector", lambda e, o=ov,
                            i0=psum[:, 512 * co:512 * (co + 1)],
                            s=ctx["boutBsb"][:, co:co + 1], i1=xv:
                            e.scalar_tensor_tensor(o, i0, s, i1, ALU.add, ALU.add),
                            inc="v")
                xring.release(ri, va)
                sch.wait("sync", va)
                st = sch.dma("sync", xdst[co][:, tsl], ov, stream=f"bo{oi}")
                orng.release(oi, st)
            evac_done[half] = va


def emit_debug_xn(ctx):
    sch, ar = ctx["sch"], ctx["ar"]
    A = OFF_A
    tmps = Ring([ar.v(A + 24576 + i * 2048, 512, F32) for i in range(2)])
    for c in range(NC):
        for k in range(NT):
            xn_t = ar.v(OFF_B + c * T * 2 + 1024 * k, 512, BF16)
            ti, tv, tf = tmps.take()
            sch.wait("vector", tf)
            cp = sch.op("vector", lambda e, o=tv, i=xn_t: e.tensor_copy(o, i), inc="v")
            sch.wait("sync", cp)
            st = sch.dma("sync", ctx["d"]["y_out"]
                         [c * 128:(c + 1) * 128, 512 * k:512 * (k + 1)], tv,
                         stream=f"dbg{ti}")
            tmps.release(ti, st)
    sch.wait_all("gpsimd", sch.barrier_tokens())


def emit_debug_hbm(ctx, chunks):
    sch, ar = ctx["sch"], ctx["ar"]
    tmps = Ring([ar.v(OFF_A + i * 16384, T, F32) for i in range(2)])
    for c in range(NC):
        ti, tv, tf = tmps.take()
        sch.wait("sync", tf)
        ld = sch.dma("sync", tv, chunks[c], stream=f"dbg{ti}")
        sch.wait("sync", ld)
        st = sch.dma("sync", ctx["d"]["y_out"][c * 128:(c + 1) * 128, :], tv, stream=f"dbg{ti}")
        tmps.release(ti, st)
    sch.wait_all("gpsimd", sch.barrier_tokens())
    sch.wait_all("sync", sch.barrier_tokens())



def kernel(**inputs):
    cores = _host_prep(inputs)
    nc = build_program()
    res = run_bass_kernel_spmd(nc, cores, core_ids=list(range(B)))
    out = np.empty((B, T, D), dtype=np.float32)
    for i in range(B):
        out[i] = res.results[i]["y_out"].T
    return out


# version 28 (final)
# speedup vs baseline: 1.0030x; 1.0030x over previous
import os
import numpy as np
import ml_dtypes
from contextlib import ExitStack

RACE_SAFE = os.environ.get("BASS_SIM_FP64") == "1"
_OWN_SEM = {"vector": "v", "scalar": "s", "tensor": "t"}

import concourse.bass as bass
import concourse.mybir as mybir
from concourse.bass_utils import run_bass_kernel_spmd

F32 = mybir.dt.float32
BF16 = mybir.dt.bfloat16
AF = mybir.ActivationFunctionType
ALU = mybir.AluOpType
AX = mybir.AxisListType

B = 8
T = 4096
D = 1024
FF = 4096
NC = D // 128
FC = FF // 128
NT = T // 512
NQ = T // 128
H = 16
HD = 64

CC_RANKS = None

M1 = float(1.5 * 2.0 ** 23)
M2 = float(1.5 * 2.0 ** 41)
M3 = float(1.5 * 2.0 ** 52)
FMAX = float(np.finfo(np.float32).max)

ENGS = ("sync", "gpsimd", "vector", "scalar", "tensor")

OFF_A = 0
OFF_B = 131072
OFF_S = 196608
AR_BYTES = 204032


class Sched:
    def __init__(self, nc, es):
        self.nc = nc
        self.es = es
        self.streams = {e: [] for e in ENGS}
        self.sem = {}
        for name in ("t", "v", "s", "cc"):
            self.sem[name] = es.enter_context(nc.semaphore("sem_" + name))
        self.cnt = {k: 0 for k in self.sem}

    def get_sem(self, name):
        if name not in self.sem:
            self.sem[name] = self.es.enter_context(
                self.nc.semaphore("sem_" + name))
            self.cnt[name] = 0
        return self.sem[name]

    def op(self, eng, fn, inc=None):
        if eng == "vector" and inc is None:
            inc = "v"
        if eng == "scalar" and inc is None:
            inc = "s"
        if eng == "vector" and self.cnt["v"] > 0:
            self.wait(eng, ("v", self.cnt["v"]))
        elif eng == "scalar" and self.cnt["s"] > 0:
            if RACE_SAFE:
                self.wait(eng, ("s", self.cnt["s"]))
        if inc is None:
            self.streams[eng].append(lambda e, fn=fn: fn(e))
            return None
        amount = 1 if inc in ("t", "v", "s", "cc") else 16
        h = self.get_sem(inc)
        def g(e, fn=fn, h=h, amount=amount):
            fn(e).then_inc(h, amount)
        self.streams[eng].append(g)
        self.cnt[inc] += amount
        return (inc, self.cnt[inc])

    def wait(self, eng, tok):
        if tok is None:
            return
        if isinstance(tok, list):
            self.wait_all(eng, tok)
            return
        name, val = tok
        h = self.get_sem(name)
        self.streams[eng].append(lambda e, h=h, val=val: e.wait_ge(h, val))

    def wait_all(self, eng, toks):
        best = {}
        def visit(tk):
            if tk is None:
                return
            if isinstance(tk, list):
                for t2 in tk:
                    visit(t2)
                return
            name, val = tk
            best[name] = max(best.get(name, 0), val)
        for tk in toks:
            visit(tk)
        for name, val in best.items():
            h = self.get_sem(name)
            self.streams[eng].append(lambda e, h=h, val=val: e.wait_ge(h, val))

    def dma(self, queue, out, in_, stream="misc", transpose=False):
        def fn(e, out=out, in_=in_, transpose=transpose):
            return e.dma_start(out=out, in_=in_, transpose=transpose)
        return self.op(queue, fn, inc="D" + stream)

    def barrier_tokens(self):
        return [(k, v) for k, v in self.cnt.items() if v > 0]

    def full_barrier(self):
        toks = self.barrier_tokens()
        for e in ENGS:
            self.wait_all(e, toks)

    def emit(self, block):
        for e in ENGS:
            stream = self.streams[e]
            if not stream:
                continue
            def runner(eng, stream=stream):
                for fn in stream:
                    fn(eng)
            getattr(block, e)(runner)


class Ring:
    def __init__(self, views):
        self.views = views
        self.n = len(views)
        self.free = [None] * self.n
        self.k = 0

    def take(self):
        i = self.k % self.n
        self.k += 1
        return i, self.views[i], self.free[i]

    def release(self, i, tok):
        self.free[i] = tok


class Arena:
    def __init__(self, handle):
        self.f32 = handle.ap()
        self.bf16 = self.f32.bitcast(BF16)

    def v(self, byte_off, n, dtype, p0=0, p1=128):
        if dtype == F32:
            assert byte_off % 4 == 0 and byte_off + 4 * n <= AR_BYTES
            return self.f32[p0:p1, byte_off // 4: byte_off // 4 + n]
        assert byte_off % 2 == 0 and byte_off + 2 * n <= AR_BYTES
        return self.bf16[p0:p1, byte_off // 2: byte_off // 2 + n]



def _rope_tables():
    import jax
    import jax.numpy as jnp
    cpu = jax.local_devices(backend="cpu")[0]
    with jax.default_device(cpu):
        i = jnp.arange(D // 2, dtype=jnp.float32)
        phi = jnp.power(10000.0, -2.0 * (i - 1.0) / D)
        pos = jnp.arange(T, dtype=jnp.float32)
        ang = pos[:, None] * phi[None, :]
        cos = np.asarray(jnp.cos(ang), dtype=np.float32)
        sin = np.asarray(jnp.sin(ang), dtype=np.float32)
    return cos, sin


def _tile_block(w):
    ri, co = w.shape
    return np.ascontiguousarray(
        w.reshape(ri // 128, 128, co // 128, 128).transpose(0, 2, 1, 3))


def _host_prep(inputs):
    f32 = np.float32
    x = np.asarray(inputs["L0_attn_a_x"], dtype=f32)
    wq = np.asarray(inputs["L0_attn_a_wq"], dtype=f32)
    wk = np.asarray(inputs["L0_attn_a_wk"], dtype=f32)
    wv = np.asarray(inputs["L0_attn_a_wv"], dtype=f32)
    wqkv = np.asarray(inputs["L0_attn_b_w_qkv"], dtype=f32)
    wout = np.asarray(inputs["L0_attn_b_w_out"], dtype=f32)
    bout = np.asarray(inputs["L0_attn_b_b_out"], dtype=f32)
    w_in = np.asarray(inputs["L0_ffn_W_in"], dtype=f32)
    b_in = np.asarray(inputs["L0_ffn_b_in"], dtype=f32)
    w_out2 = np.asarray(inputs["L0_ffn_W_out"], dtype=f32)
    b_out2 = np.asarray(inputs["L0_ffn_b_out"], dtype=f32)

    perm = np.concatenate([np.arange(0, D, 2), np.arange(1, D, 2)])
    wqkvT = np.ascontiguousarray(wqkv.T)

    shared = {
        "wqTt": _tile_block(np.ascontiguousarray(wq[perm, :].T)),
        "wkTt": _tile_block(np.ascontiguousarray(wk[perm, :].T)),
        "wvTc": np.ascontiguousarray(wv.T.reshape(NC, 128, D)),
        "wqkvTqt": _tile_block(np.ascontiguousarray(wqkvT[:, 0:D])),
        "wqkvTkv": np.ascontiguousarray(wqkvT[:, D:3 * D].reshape(NC, 128, 2 * D)),
        "woutTc": np.ascontiguousarray(wout.T.reshape(NC, 128, D)),
        "w_inT": np.ascontiguousarray(w_in.T.reshape(FC, 128, D)),
        "w_out2": np.ascontiguousarray(w_out2.reshape(FC, 128, D)),
        "b_in": np.ascontiguousarray(b_in.reshape(FC, 128, 1)),
        "b_out2": np.ascontiguousarray(b_out2.reshape(NC, 128, 1)),
        "boutB": np.ascontiguousarray(bout.reshape(NC, 128, 1)),
    }

    cos, sin = _rope_tables()
    shared["cosT"] = np.ascontiguousarray(cos.T)
    shared["sinT"] = np.ascontiguousarray(sin.T)

    jj = np.arange(896)[None, :]
    qq = np.arange(128)[:, None]
    shared["maskstrip"] = np.where(jj <= 384 + qq, FMAX, -FMAX).astype(f32)
    shared["ident"] = np.eye(128, dtype=f32).astype(ml_dtypes.bfloat16)

    ind = np.zeros((H, NC * 128), dtype=f32)
    for c in range(NC):
        ind[2 * c, c * 128:c * 128 + 64] = 1.0
        ind[2 * c + 1, c * 128 + 64:c * 128 + 128] = 1.0
    shared["ind"] = np.ascontiguousarray(ind)

    cores = []
    for i in range(B):
        m = dict(shared)
        m["x_in"] = np.ascontiguousarray(x[i].T)
        cores.append(m)
    return cores



def build_program(stop_after="all"):
    nc = bass.Bass("TRN2", target_bir_lowering=False)

    dd = {}
    def din(name, shape, dtype=F32):
        dd[name] = nc.dram_tensor(name, shape, dtype, kind="ExternalInput").ap()
    din("x_in", [D, T])
    din("wqTt", [NC, NC, 128, 128])
    din("wkTt", [NC, NC, 128, 128])
    din("wvTc", [NC, 128, D])
    din("wqkvTqt", [NC, NC, 128, 128])
    din("wqkvTkv", [NC, 128, 2 * D])
    din("woutTc", [NC, 128, D])
    din("w_inT", [FC, 128, D])
    din("w_out2", [FC, 128, D])
    din("b_in", [FC, 128, 1])
    din("b_out2", [NC, 128, 1])
    din("boutB", [NC, 128, 1])
    din("cosT", [D // 2, T])
    din("sinT", [D // 2, T])
    din("maskstrip", [128, 896])
    din("ident", [128, 128], BF16)
    din("ind", [H, NC * 128])
    dd["y_out"] = nc.dram_tensor("y_out", [D, T], F32, kind="ExternalOutput").ap()

    dd["x_state"] = nc.dram_tensor("x_state", [NC, 128, T], F32).ap()
    dd["qr_sp"] = nc.dram_tensor("qr_sp", [NC, 128, T], BF16).ap()
    dd["w1p"] = nc.dram_tensor("w1p", [FC, NC, 128, 128], BF16).ap()
    dd["qwout"] = nc.dram_tensor("qwout", [FC, 128, D], BF16).ap()
    dd["hid"] = nc.dram_tensor("hid", [FC, NT, 128, 512], BF16).ap()
    dd["cc"] = [(nc.dram_tensor(f"cc_in{i}", [1, 8], F32).ap(),
                 nc.dram_tensor(f"cc_out{i}", [1, 8], F32).ap()) for i in range(4)]

    with ExitStack() as es:
        ar = Arena(es.enter_context(nc.sbuf_tensor("arena", [128, AR_BYTES // 4], F32)))
        smalls = es.enter_context(nc.sbuf_tensor("smalls", [128, 208], F32)).ap()
        sb_ident = es.enter_context(nc.sbuf_tensor("sb_ident", [128, 128], BF16)).ap()
        sb_onesb = es.enter_context(nc.sbuf_tensor("sb_onesb", [128, 8], BF16)).ap()
        sb_onesrow = es.enter_context(nc.sbuf_tensor("sb_onesrow", [1, 128], F32)).ap()
        sb_blk = es.enter_context(nc.sbuf_tensor("sb_blk", [128, H * NC], BF16)).ap()
        sb_kv = es.enter_context(nc.sbuf_tensor("sb_kv", [128, D], BF16)).ap()
        sb_invw = es.enter_context(nc.sbuf_tensor("sb_invw", [128, 2 * FC], F32)).ap()
        sb_bin = es.enter_context(nc.sbuf_tensor("sb_bin", [128, FC], F32)).ap()
        sb_bout2 = es.enter_context(nc.sbuf_tensor("sb_bout2", [128, NC], F32)).ap()
        sb_boutB = es.enter_context(nc.sbuf_tensor("sb_boutB", [128, NC], F32)).ap()
        sb_vecs = es.enter_context(nc.sbuf_tensor("sb_vecs", [128, 64], F32)).ap()
        sb_ind = es.enter_context(nc.sbuf_tensor("sb_ind", [H, NC * 128], F32)).ap()
        psum_h = es.enter_context(nc.psum_tensor("psum", [128, 4096], F32))
        psum = psum_h.ap()
        psum_bf = psum.bitcast(BF16)

        sch = Sched(nc, es)
        for ci_, val in enumerate((M2, -M2, M3, -M3, 1e-5, 4.844187086458591)):
            nc.const_aps.register(F32, val, sb_vecs[:, 40 + ci_:41 + ci_])
        ctx = dict(nc=nc, sch=sch, ar=ar, psum=psum, psum_bf=psum_bf, smalls=smalls,
                   ident=sb_ident, onesb=sb_onesb, onesrow=sb_onesrow,
                   blk=sb_blk, kvsb=sb_kv, invw=sb_invw, binsb=sb_bin,
                   bout2sb=sb_bout2, boutBsb=sb_boutB,
                   vecs=sb_vecs, indsb=sb_ind, d=dd)

        emit_all(ctx, stop_after)

        with nc.Block() as block:
            sch.emit(block)
    return nc


def _chunks(ap3):
    return [ap3[c] for c in range(NC)]


def emit_all(ctx, stop_after):
    sch, d = ctx["sch"], ctx["d"]
    x_in_chunks = [d["x_in"][c * 128:(c + 1) * 128, :] for c in range(NC)]
    xs_chunks = _chunks(d["x_state"])
    y_chunks = [d["y_out"][c * 128:(c + 1) * 128, :] for c in range(NC)]

    emit_setup(ctx)
    emit_wquant(ctx)
    sch.full_barrier()
    emit_ln(ctx, src=x_in_chunks, amax_col=None)
    sch.full_barrier()
    if stop_after == "ln1":
        emit_debug_xn(ctx)
        return
    emit_attna(ctx, x_in_chunks, xs_chunks)
    sch.full_barrier()
    if stop_after == "attna":
        emit_debug_hbm(ctx, xs_chunks)
        return
    emit_ln(ctx, src=xs_chunks, amax_col=0)
    sch.full_barrier()
    emit_ffn(ctx, app=0, xsrc=xs_chunks, xdst=xs_chunks)
    sch.full_barrier()
    if stop_after == "ffn1":
        emit_debug_hbm(ctx, xs_chunks)
        return
    emit_ln(ctx, src=xs_chunks, amax_col=None)
    sch.full_barrier()
    if stop_after == "ln3":
        emit_debug_xn(ctx)
        return
    emit_attnb(ctx, xs_chunks, xs_chunks)
    sch.full_barrier()
    if stop_after == "attnb":
        emit_debug_hbm(ctx, xs_chunks)
        return
    emit_ln(ctx, src=xs_chunks, amax_col=0)
    sch.full_barrier()
    emit_ffn(ctx, app=1, xsrc=xs_chunks, xdst=y_chunks)
    sch.full_barrier()


def emit_setup(ctx):
    sch, d = ctx["sch"], ctx["d"]
    for ci_, val in enumerate((M2, -M2, M3, -M3, 1e-5, 4.844187086458591)):
        sch.op("vector", lambda e, ap=ctx["vecs"][:, 40 + ci_:41 + ci_], v=val:
               e.memset(ap, v))
    ctx["ident_tok"] = sch.dma("sync", ctx["ident"], d["ident"], stream="ident")
    sch.dma("sync", ctx["indsb"], d["ind"])
    for c in range(FC):
        sch.dma("sync", ctx["binsb"][:, c:c + 1], d["b_in"][c])
    for c in range(NC):
        sch.dma("sync", ctx["bout2sb"][:, c:c + 1], d["b_out2"][c])
        sch.dma("sync", ctx["boutBsb"][:, c:c + 1], d["boutB"][c])
    sch.op("vector", lambda e, ap=ctx["onesb"]: e.memset(ap, 1.0))
    sch.op("vector", lambda e, ap=ctx["onesrow"]: e.memset(ap, 1.0))
    sch.op("vector", lambda e, ap=ctx["blk"]: e.memset(ap, 0.0), inc="v")


def emit_wquant(ctx):
    sch, ar, d = ctx["sch"], ctx["ar"], ctx["d"]
    A = OFF_A
    NDEEP = 3
    inring = Ring([ar.v(A + i * 4096, D, F32) for i in range(4)])
    amg = ar.v(A + 16384, 64, F32)
    scg = ar.v(A + 16384 + 256, 64, F32)
    ubs = [ar.v(A + 20480 + i * 4096, D, F32) for i in range(NDEEP)]
    tas = [ar.v(A + 32768 + i * 4096, D, F32) for i in range(NDEEP)]
    tms = [ar.v(A + 45056 + i * 4096, D, F32) for i in range(NDEEP)]
    qb = Ring([ar.v(A + 57344 + i * 2048, D, BF16) for i in range(NDEEP)])
    tstage = Ring([ar.v(A + 63488 + i * 1024, 512, BF16) for i in range(4)])

    sch.wait("tensor", ctx["ident_tok"])
    LN127 = 4.844187086458591
    ub_free = [None] * NDEEP
    tbank_free = [None, None]
    for w2 in range(2):
        src2 = d["w_inT"] if w2 == 0 else d["w_out2"]
        for c in range(FC):
            pp = (w2 * FC + c) % NDEEP
            gidx = w2 * 32 + c
            ri, rv, rf = inring.take()
            q = ("sync", "gpsimd")[ri % 2]
            sch.wait(q, rf)
            ld = sch.dma(q, rv, src2[c], stream=f"in{ri}")
            ga = amg[:, gidx:gidx + 1]
            gs = scg[:, gidx:gidx + 1]
            sch.wait("vector", ld)
            sch.op("vector", lambda e, o=ga, i=rv: e.reduce_max(
                o, i, axis=AX.X, apply_absolute_value=True))
            r2 = sch.op("vector", lambda e, o=ga, i=ga: e.tensor_scalar(
                o, i, 1e-8, None, ALU.max))
            r3 = sch.op("vector", lambda e, o=ctx["invw"][:, gidx:gidx + 1],
                        i=ga: e.tensor_scalar(o, i, 1.0 / 127.0, None, ALU.mult))
            sch.wait("scalar", r2)
            sch.op("scalar", lambda e, o=gs, i=ga: e.activation(
                o, i, AF.Ln, bias=0.0, scale=1.0))
            s1 = sch.op("scalar", lambda e, o=gs, i=gs: e.activation(
                o, i, AF.Exp, bias=LN127, scale=-1.0))
            sch.wait("scalar", [ld, ub_free[pp]])
            su = sch.op("scalar", lambda e, o=ubs[pp], i=rv, s=gs:
                        e.activation(o, i, AF.Identity, bias=0.0, scale=s))
            inring.release(ri, su)
            if RACE_SAFE:
                sm1 = sch.op("scalar", lambda e, o=tms[pp], i=ubs[pp]: e.activation(
                    o, i, AF.Identity, bias=M2, scale=1.0))
                sm2 = sch.op("scalar", lambda e, o=tms[pp], i=tms[pp]: e.activation(
                    o, i, AF.Identity, bias=-M2, scale=1.0))
                sch.wait("vector", su)
                sch.op("vector", lambda e, o=tas[pp], i=ubs[pp]: e.tensor_scalar(
                    o, i, M3, M3, ALU.add, ALU.subtract))
                sch.wait("vector", sm1)
                v4 = sch.op("vector", lambda e, o=ubs[pp], i=ubs[pp]: e.tensor_scalar(
                    o, i, M1, M1, ALU.add, ALU.subtract))
                sch.op("vector", lambda e, o=tas[pp], a=tas[pp], b=ubs[pp]:
                       e.tensor_tensor(o, a, b, ALU.add))
                qi, qv, qf = qb.take()
                sch.wait("vector", [qf, sm2])
                v5 = sch.op("vector", lambda e, o=qv, a=tas[pp], b=tms[pp]:
                            e.tensor_tensor(o, a, b, ALU.subtract))
            else:
                qi, qv, qf = qb.take()
                sch.wait("vector", [qf, su])
                v5 = sch.op("vector", lambda e, o=qv, i=ubs[pp]: e.tensor_scalar(
                    o, i, M1, M1, ALU.add, ALU.subtract))
            ub_free[pp] = v5
            if w2 == 1:
                sch.wait("gpsimd", v5)
                st = sch.dma("gpsimd", d["qwout"][c], qv, stream=f"qst{qi}")
                qb.release(qi, st)
            else:
                last = None
                for hh in range(2):
                    pb = ctx["psum_bf"][:, 4096 * hh: 4096 * hh + 512]
                    ti, tv, tf = tstage.take()
                    sch.wait("tensor", [v5, tf, tbank_free[hh]])
                    tts = None
                    for jj in range(4):
                        ci = 4 * hh + jj
                        tts = sch.op("tensor", lambda e,
                                     o=pb[:, 128 * jj:128 * (jj + 1)],
                                     i=qv[:, 128 * ci:128 * (ci + 1)],
                                     idn=ctx["ident"]: e.transpose(o, i, idn),
                                     inc="t" if jj == 3 else None)
                    sch.wait("scalar", tts)
                    cp = sch.op("scalar", lambda e, o=tv, i=pb: e.copy(o, i))
                    tbank_free[hh] = cp
                    sch.wait("sync", cp)
                    dst3 = d["w1p"][c].rearrange("j p q -> p j q")[:, 4 * hh:4 * hh + 4, :]
                    src3 = tv.rearrange("p (j q) -> p j q", j=4)
                    st = sch.dma("sync", dst3, src3, stream=f"wst{ti}")
                    tstage.release(ti, [st, tts])
                    last = st
                qb.release(qi, last)


def emit_ln(ctx, src, amax_col=None):
    sch, ar, psum = ctx["sch"], ctx["ar"], ctx["psum"]
    onesb, onesrow = ctx["onesb"], ctx["onesrow"]
    A = OFF_A
    TW = 2048
    NG = T // TW
    xgrp = [ar.v(A + c * 8192, TW, F32) for c in range(NC)]
    bfring = Ring([ar.v(A + 65536 + i * 4096, TW, BF16) for i in range(3)])
    sqring = Ring([ar.v(A + 77824 + i * 4096, TW, BF16) for i in range(3)])
    mB = ar.v(A + 90112, TW, F32)
    rB = ar.v(A + 98304, TW, F32)
    tsc = ar.v(A + 106496, TW, F32)
    r_r = ar.v(A + 114688, TW, F32, p0=0, p1=1)
    r_b = ar.v(A + 122880, TW, F32, p0=0, p1=1)
    amax = None if amax_col is None else ctx["vecs"][:, amax_col:amax_col + 1]
    atmp = ctx["vecs"][:, 8:9]
    if amax is not None:
        sch.op("vector", lambda e, ap=amax: e.memset(ap, 0.0))

    xgate = [None] * NC
    evac_done = None
    row_free = None
    prev_apply = None

    for g in range(NG):
        gsl = slice(TW * g, TW * (g + 1))
        stats_last = None
        for c in range(NC):
            sch.wait("sync", xgate[c])
            ld = sch.dma("sync", xgrp[c], src[c][:, gsl], stream=f"lnx{c}")
            bi, bv, bf_ = bfring.take()
            sch.wait("scalar", [bf_, ld])
            cb = sch.op("scalar", lambda e, o=bv, i=xgrp[c]: e.activation(
                o, i, AF.Identity, bias=0.0, scale=1.0))
            si, sv, sf = sqring.take()
            sch.wait("scalar", sf)
            sq = sch.op("scalar", lambda e, o=sv, i=bv: e.activation(
                o, i, AF.Square, bias=0.0, scale=1.0))
            sch.wait("tensor", sq)
            if c == 0:
                sch.wait("tensor", evac_done)
            m2 = None
            for s in range(4):
                ssl = slice(512 * s, 512 * (s + 1))
                m1 = sch.op("tensor", lambda e, o=psum[0:1, ssl], l=onesb[:, 0:1],
                            r=bv[:, ssl], st=(c == 0), sp=(c == NC - 1):
                            e.matmul(o, l, r, start=st, stop=sp),
                            inc="t" if s == 3 else None)
                m2 = sch.op("tensor", lambda e, o=psum[0:1, 2048 + 512 * s:2048 + 512 * (s + 1)],
                            l=onesb[:, 0:1], r=sv[:, ssl],
                            st=(c == 0), sp=(c == NC - 1):
                            e.matmul(o, l, r, start=st, stop=sp),
                            inc="t" if s == 3 else None)
            bfring.release(bi, [m1, sq])
            sqring.release(si, m2)
            stats_last = m2

        sum_row = psum[0:1, 0:2048]
        sq_row = psum[0:1, 2048:4096]
        sch.wait("vector", [stats_last, row_free])
        sch.op("vector", lambda e, o=sum_row, i=sum_row: e.tensor_scalar(
            o, i, 1.0 / D, None, ALU.mult))
        v1 = sch.op("vector", lambda e, o=sq_row, i=sq_row: e.tensor_scalar(
            o, i, 1.0 / D, None, ALU.mult))
        sch.wait("scalar", v1)
        s1 = sch.op("scalar", lambda e, o=r_r, i=sum_row: e.activation(
            o, i, AF.Square, bias=0.0, scale=1.0))
        sch.wait("vector", s1)
        v2 = sch.op("vector", lambda e, o=r_r, a=sq_row, b=r_r: e.tensor_tensor(
            o, a, b, ALU.subtract))
        sch.wait("scalar", v2)
        sch.op("scalar", lambda e, o=r_r, i=r_r: e.activation(
            o, i, AF.Ln, bias=1e-5, scale=1.0))
        s3 = sch.op("scalar", lambda e, o=r_r, i=r_r: e.activation(
            o, i, AF.Exp, bias=0.0, scale=-0.5))
        sch.wait("vector", s3)
        v3 = sch.op("vector", lambda e, o=r_b, a=sum_row, b=r_r: e.tensor_tensor(
            o, a, b, ALU.mult))
        sch.wait("tensor", v3)
        bc_last = None
        for s in range(4):
            sch.op("tensor", lambda e, o=psum[:, 512 * s:512 * (s + 1)],
                   l=onesrow, r=r_b[0:1, 512 * s:512 * (s + 1)]:
                   e.matmul(o, l, r, start=True, stop=True))
            bc_last = sch.op("tensor", lambda e,
                             o=psum[:, 2048 + 512 * s:2048 + 512 * (s + 1)],
                             l=onesrow, r=r_r[0:1, 512 * s:512 * (s + 1)]:
                             e.matmul(o, l, r, start=True, stop=True),
                             inc="t" if s == 3 else None)
        row_free = bc_last
        sch.wait("scalar", [bc_last, prev_apply])
        ev_m = sch.op("scalar", lambda e, o=mB, i=psum[:, 0:2048]: e.copy(o, i),
                      inc="s")
        evac_done = sch.op("scalar", lambda e, o=rB, i=psum[:, 2048:4096]:
                           e.copy(o, i))

        va = None
        sch.wait("vector", ev_m)
        for c in range(NC):
            xn_t = ar.v(OFF_B + c * (T * 2) + 2 * TW * g, TW, BF16)
            sch.op("vector", lambda e, o=tsc, a=xgrp[c], b=mB: e.tensor_tensor(
                o, a, b, ALU.subtract))
            if c == 0:
                sch.wait("vector", evac_done)
            va = sch.op("vector", lambda e, o=xn_t, a=tsc, b=rB: e.tensor_tensor(
                o, a, b, ALU.mult))
            if amax is not None:
                sch.op("vector", lambda e, o=atmp, i=xn_t: e.reduce_max(
                    o, i, axis=AX.X, apply_absolute_value=True))
                va = sch.op("vector", lambda e, o=amax, a=amax, b=atmp:
                            e.tensor_tensor(o, a, b, ALU.max))
            xgate[c] = va
        prev_apply = va


def emit_attna(ctx, xres_chunks, xout_chunks):
    sch, ar, psum, d = ctx["sch"], ctx["ar"], ctx["psum"], ctx["d"]
    xn = [ar.v(OFF_B + c * T * 2, T, BF16) for c in range(NC)]
    kr = [ar.v(OFF_A + c * T * 2, T, BF16) for c in range(NC)]
    vtm = [ar.v(OFF_A + 65536 + kb * 2048, D, BF16) for kb in range(NQ)]
    cosb = ar.v(OFF_A + 65536, T, F32)
    sinb = ar.v(OFF_A + 65536 + 16384, T, F32)
    AH = OFF_A + 98304
    wring = Ring([ar.v(AH + i * 256, 128, BF16) for i in range(32)])
    t1 = ar.v(AH + 8192, 512, F32)
    t2 = ar.v(AH + 10240, 512, F32)
    qstage = Ring([ar.v(AH + 12288 + i * 1024, 512, BF16) for i in range(4)])
    S = OFF_S
    wvring = Ring([ar.v(S + i * 2048, D, BF16) for i in range(2)])

    pairs = [(d["wqTt"], True, p) for p in range(4)] + \
            [(d["wkTt"], False, p) for p in range(4)]

    def load_pair(idx):
        wsrc, _, p = pairs[idx]
        toks, views = [], {}
        for half, co in enumerate((p, p + 4)):
            for ci in range(NC):
                wi, wv_, wf = wring.take()
                sch.wait("gpsimd", wf)
                wl = sch.dma("gpsimd", wv_, wsrc[ci, co], stream=f"wq{idx % 2}")
                views[(half, ci)] = (wi, wv_)
                toks.append(wl)
        return toks, views

    prev_ro = None
    banks = [psum[:, 0:512], psum[:, 512:1024],
             psum[:, 1024:1536], psum[:, 1536:2048]]
    bank_free = [None] * 4
    pl = load_pair(0)
    for idx in range(8):
        wsrc, is_q, p = pairs[idx]
        if True:
            wtoks, wviews0 = pl
            wviews = {kk: vv[1] for kk, vv in wviews0.items()}
            sch.wait("sync", prev_ro)
            cl = sch.dma("sync", cosb, d["cosT"][128 * p:128 * (p + 1), :], stream="tab")
            sl = sch.dma("sync", sinb, d["sinT"][128 * p:128 * (p + 1), :], stream="tab")
            if idx < 7:
                pl = load_pair(idx + 1)
            sch.wait_all("tensor", wtoks)
            sch.wait_all("vector", [cl, sl])
            ro = None
            for k in range(NT):
                tsl = slice(512 * k, 512 * (k + 1))
                pe = banks[(k % 2) * 2]
                po = banks[(k % 2) * 2 + 1]
                sch.wait("tensor", bank_free[(k % 2) * 2])
                for ci in range(NC):
                    sch.op("tensor", lambda e, o=pe, l=wviews[(0, ci)],
                           r=xn[ci][:, tsl], st=(ci == 0), sp=(ci == NC - 1):
                           e.matmul(o, l, r, start=st, stop=sp))
                    mmo = sch.op("tensor", lambda e, o=po, l=wviews[(1, ci)],
                                 r=xn[ci][:, tsl], st=(ci == 0), sp=(ci == NC - 1):
                                 e.matmul(o, l, r, start=st, stop=sp),
                                 inc="t" if ci == NC - 1 else None)
                sch.wait("vector", mmo)
                cosv = cosb[:, tsl]
                sinv = sinb[:, tsl]
                if is_q:
                    qi_e, qv_e, qf_e = qstage.take()
                    sch.wait("vector", qf_e)
                    dst_e = qv_e
                    qi_o, qv_o, qf_o = qstage.take()
                    sch.wait("vector", qf_o)
                    dst_o = qv_o
                else:
                    dst_e = kr[p][:, tsl]
                    dst_o = kr[p + 4][:, tsl]
                sch.op("vector", lambda e, o=t1, a=pe, b=cosv: e.tensor_tensor(
                    o, a, b, ALU.mult))
                sch.op("vector", lambda e, o=t2, a=po, b=sinv: e.tensor_tensor(
                    o, a, b, ALU.mult))
                sch.op("vector", lambda e, o=dst_e, a=t1, b=t2: e.tensor_tensor(
                    o, a, b, ALU.add))
                sch.op("vector", lambda e, o=t1, a=pe, b=sinv: e.tensor_tensor(
                    o, a, b, ALU.mult))
                sch.op("vector", lambda e, o=t2, a=po, b=cosv: e.tensor_tensor(
                    o, a, b, ALU.mult))
                ro = sch.op("vector", lambda e, o=dst_o, a=t2, b=t1: e.tensor_tensor(
                    o, a, b, ALU.subtract), inc="v")
                bank_free[(k % 2) * 2] = ro
                bank_free[(k % 2) * 2 + 1] = ro
                if is_q:
                    sch.wait("sync", ro)
                    s1 = sch.dma("sync", d["qr_sp"][p][:, tsl], dst_e, stream=f"qs{qi_e}")
                    s2 = sch.dma("sync", d["qr_sp"][p + 4][:, tsl], dst_o, stream=f"qs{qi_o}")
                    qstage.release(qi_e, s1)
                    qstage.release(qi_o, s2)
            prev_ro = ro
            for (wi, _) in wviews0.values():
                wring.free[wi] = ro

    vbanks = [psum[:, 0:512], psum[:, 512:1024],
              psum[:, 1024:1536], psum[:, 1536:2048]]
    vfree = [prev_ro] * 4
    wvv1 = [ar.v(OFF_A + 65536 + (24 + i) * 2048, D, BF16) for i in range(NC)]
    wvres = [ar.v(OFF_B + ci * T * 2, D, BF16) for ci in range(NC)]
    wtoks = []
    for ci in range(NC):
        wtoks.append(sch.dma("gpsimd", wvv1[ci], d["wvTc"][ci], stream="wvA"))
    sch.wait_all("tensor", wtoks)
    sch.wait_all("vector", list(qstage.free))

    def v_tile(kb, wv_of_ci):
        ksl = slice(128 * kb, 128 * (kb + 1))
        b0 = vbanks[(kb % 2) * 2]
        b1 = vbanks[(kb % 2) * 2 + 1]
        sch.wait("tensor", vfree[(kb % 2) * 2])
        vm = None
        for ci in range(NC):
            wv_ = wv_of_ci(ci)
            sch.op("tensor", lambda e, o=b0, l=xn[ci][:, ksl], r=wv_[:, 0:512],
                   st=(ci == 0), sp=(ci == NC - 1):
                   e.matmul(o, l, r, start=st, stop=sp))
            vm = sch.op("tensor", lambda e, o=b1, l=xn[ci][:, ksl],
                        r=wv_[:, 512:1024], st=(ci == 0), sp=(ci == NC - 1):
                        e.matmul(o, l, r, start=st, stop=sp),
                        inc="t" if ci == NC - 1 else None)
        sch.wait("vector", vm)
        sch.op("vector", lambda e, o=vtm[kb][:, 0:512], i=b0: e.tensor_copy(o, i))
        ve = sch.op("vector", lambda e, o=vtm[kb][:, 512:1024], i=b1:
                    e.tensor_copy(o, i), inc="v")
        vfree[(kb % 2) * 2] = ve
        vfree[(kb % 2) * 2 + 1] = ve
        return vm, ve

    for kb in range(24):
        vm, ve = v_tile(kb, lambda ci: wvv1[ci])
    sch.wait("vector", vm)
    mig = []
    for ci in range(NC):
        mig.append(sch.op("vector", lambda e, o=wvres[ci], i=wvv1[ci]:
                          e.tensor_copy(o, i), inc="v"))
    sch.wait_all("tensor", mig)
    for kb in range(24, NQ):
        vm, ve = v_tile(kb, lambda ci: wvres[ci])

    sch.full_barrier()

    qg = [ar.v(OFF_B + c * 1024, 512, BF16) for c in range(NC)]
    probs = ar.v(OFF_B + 8192, T, BF16)
    PT = ar.v(OFF_B + 16384, NQ * 512, BF16)
    masks = ar.v(OFF_B + 49152, 896, F32)
    xring = Ring([ar.v(OFF_B + 53248 + i * 2048, 512, F32) for i in range(2)])
    oring = Ring([ar.v(OFF_B + 57344 + i * 2048, 512, F32) for i in range(2)])
    mtok = sch.dma("sync", masks, d["maskstrip"], stream="misc")
    sch.wait("vector", mtok)

    ident = ctx["ident"]
    mx = ctx["vecs"][:, 9:10]
    bias = ctx["vecs"][:, 10:11]
    ssum = ctx["vecs"][:, 11:12]
    rsum = ctx["vecs"][:, 12:13]

    bank_free = [None] * 8
    halftog = 0
    qg_free = None
    for g in range(NT):
        qsl = slice(512 * g, 512 * (g + 1))
        sch.wait("sync", qg_free)
        qltoks = []
        for c in range(NC):
            qltoks.append(sch.dma("sync", qg[c], d["qr_sp"][c][:, qsl], stream="qg"))
        sch.wait_all("tensor", qltoks)
        pt_ready = []

        def emit_tp(j, nkb512, pbase, vn):
            tbase = pbase if nkb512 <= 4 else 2048
            tb0 = tbase // 512
            nkb = 4 * nkb512
            sch.wait("tensor", vn)
            for kq in range(0, nkb, 4):
                nn = min(4, nkb - kq)
                tts = None
                for u2 in range(nn):
                    kb = kq + u2
                    sl4 = kb % 4
                    c0 = 2 * (tbase + 512 * sl4)
                    bslot = ctx["psum_bf"][:, c0:c0 + 128]
                    sch.wait("tensor", bank_free[tb0 + sl4])
                    tts = sch.op("tensor", lambda e, o=bslot,
                                 i=probs[:, 128 * kb:128 * (kb + 1)], idn=ident:
                                 e.transpose(o, i, idn),
                                 inc="t" if u2 == nn - 1 else None)
                sch.wait("vector", tts)
                psrc = ctx["psum_bf"][:, 2 * tbase:2 * tbase + 4096]
                psrc3 = psrc.rearrange("p (s r) -> p s r", s=4)[:, 0:nn, 0:128]
                pd = PT[:, 512 * kq:512 * (kq + 4)]
                pd3 = pd.rearrange("p (s r) -> p s r", s=4)[:, 0:nn,
                                                           128 * j:128 * (j + 1)]
                cp = sch.op("vector", lambda e, o=pd3, i=psrc3: e.tensor_copy(o, i))
                for u2 in range(nn):
                    bank_free[tb0 + (kq + u2) % 4] = cp
                pt_ready.append(cp)

        pending_tp = None
        for j in range(4):
            qi = 4 * g + j
            nkb512 = qi // 4 + 1
            Lb = 512 * nkb512
            if nkb512 <= 4:
                hh = halftog
                halftog ^= 1
                pbase = 2048 * hh
            else:
                pbase = 0
            b0 = pbase // 512
            mm = None
            flush_after_first = pending_tp
            for kb5 in range(nkb512):
                sch.wait("tensor", bank_free[b0 + kb5])
                for ci in range(NC):
                    mm = sch.op("tensor", lambda e,
                                o=psum[:, pbase + 512 * kb5:pbase + 512 * (kb5 + 1)],
                                l=qg[ci][:, 128 * j:128 * (j + 1)],
                                r=kr[ci][:, 512 * kb5:512 * (kb5 + 1)],
                                st=(ci == 0), sp=(ci == NC - 1):
                                e.matmul(o, l, r, start=st, stop=sp),
                                inc="t" if ci == NC - 1 else None)
                if kb5 == 0 and flush_after_first is not None:
                    emit_tp(*flush_after_first)
                    flush_after_first = None
                    pending_tp = None
                if kb5 < nkb512 - 1:
                    sch.wait("vector", mm)
                    if kb5 == 0:
                        sch.op("vector", lambda e, o=mx,
                               i=psum[:, pbase + 512 * kb5:pbase + 512 * (kb5 + 1)]:
                               e.reduce_max(o, i, axis=AX.X))
                    else:
                        sch.op("vector", lambda e, o=bias,
                               i=psum[:, pbase + 512 * kb5:pbase + 512 * (kb5 + 1)]:
                               e.reduce_max(o, i, axis=AX.X))
                        sch.op("vector", lambda e, o=mx, a=mx, b=bias:
                               e.tensor_tensor(o, a, b, ALU.max))
            sch.wait("vector", mm)
            moff = 384 - 128 * (qi % 4)
            dsl = slice(pbase + 512 * (nkb512 - 1), pbase + Lb)
            sch.op("vector", lambda e, o=psum[:, dsl], a=psum[:, dsl],
                   b=masks[:, moff:moff + 512]: e.tensor_tensor(o, a, b, ALU.min))
            sch.op("vector", lambda e, o=bias, i=psum[:, dsl]: e.reduce_max(
                o, i, axis=AX.X))
            if nkb512 > 1:
                sch.op("vector", lambda e, o=mx, a=mx, b=bias: e.tensor_tensor(
                    o, a, b, ALU.max))
            else:
                sch.op("vector", lambda e, o=mx, i=bias: e.tensor_copy(o, i))
            vb = sch.op("vector", lambda e, o=bias, i=mx: e.tensor_scalar(
                o, i, -1.0 / 32.0, None, ALU.mult))
            sch.wait("scalar", vb)
            se = sch.op("scalar", lambda e, o=probs[:, 0:Lb],
                        i=psum[:, pbase:pbase + Lb],
                        b2=bias: e.activation(o, i, AF.Exp, bias=b2, scale=1.0 / 32.0,
                                              accum_out=ssum))
            for bb_ in range(nkb512):
                bank_free[b0 + bb_] = se
            sch.op("scalar", lambda e, o=rsum, i=ssum: e.activation(
                o, i, AF.Ln, bias=0.0, scale=1.0))
            s6 = sch.op("scalar", lambda e, o=rsum, i=rsum: e.activation(
                o, i, AF.Exp, bias=0.0, scale=-1.0))
            sch.wait("vector", [se, s6])
            vn = sch.op("vector", lambda e, o=probs[:, 0:Lb], i=probs[:, 0:Lb],
                        s=rsum: e.tensor_scalar(o, i, s, None, ALU.mult))
            pending_tp = (j, nkb512, pbase, vn)
        if pending_tp is not None:
            emit_tp(*pending_tp)
            pending_tp = None
        sch.wait_all("tensor", pt_ready)
        nkb = 4 * (g + 1)
        va = None
        for dc in range(NC):
            sch.wait("tensor", bank_free[dc])
            mmv = None
            for kb in range(nkb):
                mmv = sch.op("tensor", lambda e, o=psum[:, 512 * dc:512 * (dc + 1)],
                             l=vtm[kb][:, 128 * dc:128 * (dc + 1)],
                             r=PT[:, 512 * kb:512 * (kb + 1)],
                             st=(kb == 0), sp=(kb == nkb - 1):
                             e.matmul(o, l, r, start=st, stop=sp),
                             inc="t" if kb == nkb - 1 else None)
            ri, xv, xf = xring.take()
            sch.wait("sync", xf)
            ld = sch.dma("sync", xv, xres_chunks[dc][:, qsl], stream=f"ax{ri}")
            oi, ov, of = oring.take()
            sch.wait("vector", [of, ld, mmv])
            va = sch.op("vector", lambda e, o=ov, a=xv,
                        b=psum[:, 512 * dc:512 * (dc + 1)]:
                        e.tensor_tensor(o, a, b, ALU.add))
            xring.release(ri, va)
            bank_free[dc] = va
            sch.wait("sync", va)
            st = sch.dma("sync", xout_chunks[dc][:, qsl], ov, stream=f"ao{oi}")
            oring.release(oi, st)
            if "xst_tok" in ctx:
                ctx["xst_tok"][dc][g] = st
        qg_free = va


def _bcast_scalar(ctx, src11, dst_col, pre_tok):
    sch, psum = ctx["sch"], ctx["psum"]
    dst = ctx["vecs"][:, dst_col:dst_col + 1]
    pb = psum[:, 3584:3592]
    sch.wait("tensor", pre_tok)
    tm = sch.op("tensor", lambda e, o=pb[:, 0:1], l=ctx["onesrow"], r=src11:
                e.matmul(o, l, r, start=True, stop=True), inc="t")
    sch.wait("vector", tm)
    vt = sch.op("vector", lambda e, o=dst, i=pb[:, 0:1]: e.tensor_copy(o, i), inc="v")
    return dst, vt


def _amax_allreduce(ctx, amax_vec, cc_idx, pre_toks):
    sch, ar, psum, d = ctx["sch"], ctx["ar"], ctx["psum"], ctx["d"]
    cc_in, cc_out = d["cc"][cc_idx]
    r8 = ar.v(OFF_S + 4096, 8, F32, p0=0, p1=1)
    g11 = ar.v(OFF_S + 4128, 1, F32, p0=0, p1=1)
    ab = ar.v(OFF_S + 4352, 1, BF16)
    pb = ctx["psum_bf"][0:1, 2 * 3592:2 * 3592 + 128]
    sch.wait_all("vector", pre_toks)
    cb = sch.op("vector", lambda e, o=ab, i=amax_vec: e.tensor_copy(o, i), inc="v")
    sch.wait("tensor", cb)
    tt = sch.op("tensor", lambda e, o=pb, i=ab, idn=ctx["ident"]:
                e.transpose(o, i, idn), inc="t")
    sch.wait("vector", tt)
    sch.op("vector", lambda e, o=r8[0:1, 0:1], i=pb: e.reduce_max(o, i, axis=AX.X))
    v1 = sch.op("vector", lambda e, o=r8[0:1, 0:1], i=r8[0:1, 0:1]:
                e.tensor_scalar(o, i, 1e-8, None, ALU.max), inc="v")
    sch.wait("vector", v1)
    v2 = sch.op("vector", lambda e, o=r8, i=r8[0:1, 0:1].to_broadcast((1, 8)):
                e.tensor_copy(o, i), inc="v")
    sch.wait("gpsimd", v2)
    d1 = sch.dma("gpsimd", cc_in, r8, stream="ccd")
    sch.wait("gpsimd", d1)
    ranks = CC_RANKS if CC_RANKS is not None else list(range(B))
    cc = sch.op("gpsimd", lambda e, i=cc_in, o=cc_out, rg=[ranks]: e.collective_compute(
        "AllReduce", ALU.max, replica_groups=rg,
        ins=[i], outs=[o]), inc="cc")
    sch.wait("gpsimd", cc)
    d2 = sch.dma("gpsimd", g11, cc_out[0:1, 0:1], stream="ccd")
    return g11, d2


def emit_ffn(ctx, app, xsrc, xdst):
    sch, ar, psum, d = ctx["sch"], ctx["ar"], ctx["psum"], ctx["d"]
    A = OFF_A
    xn = [ar.v(OFF_B + c * T * 2, T, BF16) for c in range(NC)]
    amax_x = ctx["vecs"][:, 0:1]

    g11, gtok = _amax_allreduce(ctx, amax_x, app * 2, [])
    sc11 = ar.v(OFF_S + 4160, 1, F32, p0=0, p1=1)
    iv11 = ar.v(OFF_S + 4192, 1, F32, p0=0, p1=1)
    sch.wait("scalar", gtok)
    s0 = sch.op("scalar", lambda e, o=sc11, i=g11: e.activation(
        o, i, AF.Ln, bias=0.0, scale=1.0))
    s1 = sch.op("scalar", lambda e, o=sc11, i=sc11: e.activation(
        o, i, AF.Exp, bias=0.0, scale=-1.0))
    sch.wait("vector", [s1, gtok])
    v2 = sch.op("vector", lambda e, o=sc11, i=sc11: e.tensor_scalar(
        o, i, 32767.0, None, ALU.mult))
    v3 = sch.op("vector", lambda e, o=iv11, i=g11: e.tensor_scalar(
        o, i, 1.0 / 32767.0, None, ALU.mult))
    scale_vec, st1 = _bcast_scalar(ctx, sc11, 1, v2)
    inv_vec, st2 = _bcast_scalar(ctx, iv11, 2, [v3, st1])

    ubs = [ar.v(A + 0 + i * 16384, T, F32) for i in range(2)]
    tas = [ar.v(A + 32768 + i * 16384, T, F32) for i in range(2)]
    tms = [ar.v(A + 65536 + i * 16384, T, F32) for i in range(2)]
    sch.wait_all("scalar", [st1, st2, v3])
    prev_q = [None, None]
    for c in range(NC):
        pp = c % 2
        tile = ar.v(OFF_B + c * T * 2, T, BF16)
        sch.wait("scalar", prev_q[pp])
        s1 = sch.op("scalar", lambda e, o=ubs[pp], i=tile, s=scale_vec: e.activation(
            o, i, AF.Identity, bias=0.0, scale=s))
        if RACE_SAFE:
            sm1 = sch.op("scalar", lambda e, o=tms[pp], i=ubs[pp]: e.activation(o, i, AF.Identity, bias=M2, scale=1.0))
            sm2 = sch.op("scalar", lambda e, o=tms[pp], i=tms[pp]: e.activation(o, i, AF.Identity, bias=-M2, scale=1.0))
            sch.wait("vector", s1)
            sch.op("vector", lambda e, o=tas[pp], i=ubs[pp]: e.tensor_scalar(
                o, i, M3, M3, ALU.add, ALU.subtract))
            sch.wait("vector", sm1)
            v4 = sch.op("vector", lambda e, o=ubs[pp], i=ubs[pp]: e.tensor_scalar(
                o, i, M1, M1, ALU.add, ALU.subtract))
            sch.op("vector", lambda e, o=tas[pp], a=tas[pp], b=ubs[pp]: e.tensor_tensor(
                o, a, b, ALU.add))
            sch.wait("vector", sm2)
            pq = sch.op("vector", lambda e, o=tile, a=tas[pp], b=tms[pp]: e.tensor_tensor(
                o, a, b, ALU.subtract))
        else:
            sch.wait("vector", s1)
            pq = sch.op("vector", lambda e, o=tile, i=ubs[pp]: e.tensor_scalar(
                o, i, M1, M1, ALU.add, ALU.subtract))
        prev_q[pp] = pq
    prev_q = [prev_q[0], prev_q[1]]

    svW1 = ar.v(A + 98304, FC, F32)
    sch.wait("vector", prev_q)
    sv_tok = None
    for Fc in range(FC):
        sv_tok = sch.op("vector", lambda e, o=svW1[:, Fc:Fc + 1],
                        a=ctx["invw"][:, Fc:Fc + 1], b=inv_vec:
                        e.tensor_tensor(o, a, b, ALU.mult),
                        inc="v" if Fc == FC - 1 else None)

    w1ring = Ring([ar.v(A + 100352 + i * 256, 128, BF16) for i in range(16)])
    hstage = Ring([ar.v(A + 104448 + i * 1024, 512, BF16) for i in range(4)])
    amax_h = ctx["vecs"][:, 3:4]
    ah_tmp = ctx["vecs"][:, 8:9]
    sch.op("vector", lambda e, ap=amax_h: e.memset(ap, 0.0), inc="v")
    sch.wait_all("tensor", [prev_q, sv_tok])
    hbanks = [psum[:, 512 * i:512 * (i + 1)] for i in range(4)]
    hfree = [None] * 4
    amtok = None
    for Fc in range(FC):
        wtoks = []
        wviews = []
        for ci in range(NC):
            wi, wv_, wf = w1ring.take()
            sch.wait("gpsimd", wf)
            wl = sch.dma("gpsimd", wv_, d["w1p"][Fc, ci], stream=f"w1{Fc % 2}")
            wviews.append((wi, wv_))
            wtoks.append(wl)
        sch.wait_all("tensor", wtoks)
        lastmm = None
        for k in range(NT):
            tsl = slice(512 * k, 512 * (k + 1))
            bk = hbanks[k % 4]
            sch.wait("tensor", hfree[k % 4])
            for ci in range(NC):
                lastmm = sch.op("tensor", lambda e, o=bk, l=wviews[ci][1],
                                r=xn[ci][:, tsl], st=(ci == 0), sp=(ci == NC - 1):
                                e.matmul(o, l, r, start=st, stop=sp),
                                inc="t" if ci == NC - 1 else None)
            hi, hv, hf = hstage.take()
            sch.wait("scalar", hf)
            sch.wait("scalar", lastmm)
            ge = sch.op("scalar", lambda e, o=hv, i=bk, s=svW1[:, Fc:Fc + 1],
                        b2=ctx["binsb"][:, Fc:Fc + 1]:
                        e.activation(o, i, AF.Gelu, bias=b2, scale=s), inc="s")
            hfree[k % 4] = ge
            sch.wait("vector", ge)
            sch.op("vector", lambda e, o=ah_tmp, i=hv: e.reduce_max(
                o, i, axis=AX.X, apply_absolute_value=True))
            amtok = sch.op("vector", lambda e, o=amax_h, a=amax_h, b=ah_tmp:
                           e.tensor_tensor(o, a, b, ALU.max), inc="v")
            sch.wait("sync", ge)
            hs = sch.dma("sync", d["hid"][Fc, k], hv, stream=f"hs{hi}")
            hstage.release(hi, [amtok, hs])
        for wi, _ in wviews:
            w1ring.free[wi] = lastmm

    g2, gtok2 = _amax_allreduce(ctx, amax_h, app * 2 + 1, [amtok])
    sc2 = ar.v(OFF_S + 4224, 1, F32, p0=0, p1=1)
    iv2 = ar.v(OFF_S + 4256, 1, F32, p0=0, p1=1)
    sch.wait("scalar", gtok2)
    s0b = sch.op("scalar", lambda e, o=sc2, i=g2: e.activation(
        o, i, AF.Ln, bias=0.0, scale=1.0))
    s1b = sch.op("scalar", lambda e, o=sc2, i=sc2: e.activation(
        o, i, AF.Exp, bias=0.0, scale=-1.0))
    sch.wait("vector", [s1b, gtok2])
    v2 = sch.op("vector", lambda e, o=sc2, i=sc2: e.tensor_scalar(
        o, i, 32767.0, None, ALU.mult))
    v3 = sch.op("vector", lambda e, o=iv2, i=g2: e.tensor_scalar(
        o, i, 1.0 / 32767.0, None, ALU.mult))
    scale_h, sh1 = _bcast_scalar(ctx, sc2, 4, v2)
    inv_h, sh2 = _bcast_scalar(ctx, iv2, 5, [v3, sh1])

    w2 = [ar.v(OFF_B + c * 2048, D, BF16) for c in range(FC)]
    vecs2 = ar.v(A + 4096, FC, F32)
    qtoks = []
    sch.wait("gpsimd", lastmm)
    for c in range(FC):
        qtoks.append(sch.dma("gpsimd", w2[c], d["qwout"][c],
                             stream=f"qw{c % 2}"))
    sch.wait_all("vector", [sh1, sh2, v3])
    vc_tok = None
    for c in range(FC):
        vc_tok = sch.op("vector", lambda e, o=vecs2[:, c:c + 1],
                        a=ctx["invw"][:, FC + c:FC + c + 1], b=inv_h:
                        e.tensor_tensor(o, a, b, ALU.mult),
                        inc="v" if c == FC - 1 else None)
    w2toks = []
    sch.wait_all("vector", qtoks)
    for c in range(FC):
        wt = sch.op("vector", lambda e, o=w2[c], i=w2[c], s=vecs2[:, c:c + 1]:
                    e.tensor_scalar(o, i, s, None, ALU.mult), inc="v")
        w2toks.append(wt)

    NDEEP = 3
    hbuf = Ring([ar.v(A + 0 + i * 4096, 2048, BF16) for i in range(3)])
    hbuf.free = [w2toks, w2toks, w2toks]
    ub2 = [ar.v(A + 12288 + i * 8192, 2048, F32) for i in range(NDEEP)]
    t64b = [ar.v(A + 36864 + i * 8192, 2048, F32) for i in range(NDEEP)]
    tab = [ar.v(A + 61440 + i * 8192, 2048, F32) for i in range(NDEEP)]
    qhk = [ar.v(A + 86016 + g * 4096, 2048, BF16) for g in range(8)]
    xring = Ring([ar.v(A + 118784 + i * 2048, 512, F32) for i in range(4)])
    orng = Ring([ar.v(A + 126976 + i * 2048, 512, F32) for i in range(2)])
    sch.wait_all("scalar", w2toks)
    sch.wait_all("tensor", w2toks)
    sch.wait_all("sync", list(hstage.free))
    ub_free2 = [None] * NDEEP
    qh_free = [None] * 8
    half_free = [None, None]
    vq_tok = [None] * 8
    pend = None
    for k in range(NT):
        tsl = slice(512 * k, 512 * (k + 1))

        def requant(cg):
            pp = cg % NDEEP
            hi, hv, hf = hbuf.take()
            sch.wait("sync", hf)
            ldt = []
            for j in range(4):
                ldt.append(sch.dma("sync", hv[:, 512 * j:512 * (j + 1)],
                                   d["hid"][4 * cg + j, k], stream=f"hr{hi}"))
            sch.wait("scalar", ldt)
            sch.wait("scalar", ub_free2[pp])
            s1 = sch.op("scalar", lambda e, o=ub2[pp], i=hv, s=scale_h: e.activation(
                o, i, AF.Identity, bias=0.0, scale=s))
            hbuf.release(hi, s1)
            if RACE_SAFE:
                s2 = sch.op("scalar", lambda e, o=t64b[pp], i=ub2[pp]: e.activation(
                    o, i, AF.Identity, bias=M3, scale=1.0))
                s3 = sch.op("scalar", lambda e, o=t64b[pp], i=t64b[pp]: e.activation(
                    o, i, AF.Identity, bias=-M3, scale=1.0))
                sch.wait("vector", s1)
                sch.op("vector", lambda e, o=tab[pp], i=ub2[pp]: e.tensor_scalar(
                    o, i, M1, M1, ALU.add, ALU.subtract))
                sch.wait("vector", s2)
                v4 = sch.op("vector", lambda e, o=ub2[pp], i=ub2[pp]: e.tensor_scalar(
                    o, i, M2, M2, ALU.add, ALU.subtract))
                sch.wait("vector", s3)
                sch.op("vector", lambda e, o=tab[pp], a=tab[pp], b=t64b[pp]:
                       e.tensor_tensor(o, a, b, ALU.add))
                sch.wait("vector", qh_free[cg])
                vq = sch.op("vector", lambda e, o=qhk[cg], a=tab[pp], b=ub2[pp]:
                            e.tensor_tensor(o, a, b, ALU.subtract))
            else:
                sch.wait("vector", [s1, qh_free[cg]])
                vq = sch.op("vector", lambda e, o=qhk[cg], i=ub2[pp]: e.tensor_scalar(
                    o, i, M1, M1, ALU.add, ALU.subtract))
            ub_free2[pp] = vq
            vq_tok[cg] = vq

        def emit_evac(half, ptsl, mm_tok):
            va = None
            for dd2 in range(4):
                dc = 4 * half + dd2
                ri, xv, xf = xring.take()
                sch.wait("sync", xf)
                ld = sch.dma("sync", xv, xsrc[dc][:, ptsl], stream=f"fx{ri}")
                oi, ov, of = orng.take()
                sch.wait("vector", [of, ld, mm_tok])
                va = sch.op("vector", lambda e, o=ov,
                            i0=psum[:, 512 * dc:512 * (dc + 1)],
                            s=ctx["bout2sb"][:, dc:dc + 1], i1=xv:
                            e.scalar_tensor_tensor(o, i0, s, i1, ALU.add, ALU.add))
                xring.release(ri, va)
                sch.wait("sync", va)
                st = sch.dma("sync", xdst[dc][:, ptsl], ov, stream=f"fo{oi}")
                orng.release(oi, st)
            half_free[half] = va

        if pend is not None:
            emit_evac(0, pend[0], pend[1])
        sch.wait("tensor", half_free[0])
        mmh = None
        for cg in range(8):
            requant(cg)
            sch.wait("tensor", vq_tok[cg])
            for j in range(4):
                c = 4 * cg + j
                for dd2 in range(4):
                    dc = dd2
                    mmh = sch.op("tensor", lambda e,
                                 o=psum[:, 512 * dc:512 * (dc + 1)],
                                 l=w2[c][:, 128 * dc:128 * (dc + 1)],
                                 r=qhk[cg][:, 512 * j:512 * (j + 1)],
                                 st=(c == 0), sp=(c == FC - 1):
                                 e.matmul(o, l, r, start=st, stop=sp),
                                 inc="t" if (c == FC - 1 and dd2 == 3) else None)
            if cg == 1 and pend is not None:
                emit_evac(1, pend[0], pend[2])
                pend = None
        mm_half = [mmh, None]
        sch.wait("tensor", half_free[1])
        for cg in range(8):
            sch.wait("tensor", vq_tok[cg])
            mmc = None
            for j in range(4):
                c = 4 * cg + j
                for dd2 in range(4):
                    dc = 4 + dd2
                    mmc = sch.op("tensor", lambda e,
                                 o=psum[:, 512 * dc:512 * (dc + 1)],
                                 l=w2[c][:, 128 * dc:128 * (dc + 1)],
                                 r=qhk[cg][:, 512 * j:512 * (j + 1)],
                                 st=(c == 0), sp=(c == FC - 1):
                                 e.matmul(o, l, r, start=st, stop=sp),
                                 inc="t" if (j == 3 and dd2 == 3) else None)
            qh_free[cg] = mmc
        mm_half[1] = mmc
        pend = (tsl, mm_half[0], mm_half[1])
    emit_evac(0, pend[0], pend[1])
    emit_evac(1, pend[0], pend[2])


def emit_attnb(ctx, xsrc, xdst):
    sch, ar, psum, d = ctx["sch"], ctx["ar"], ctx["psum"], ctx["d"]
    xn = [ar.v(OFF_B + c * T * 2, T, BF16) for c in range(NC)]
    qphi = [ar.v(OFF_A + c * T * 2, T, BF16) for c in range(NC)]
    AH = OFF_A + 65536
    mixer = [ar.v(AH + c * T * 2, T, BF16) for c in range(NC)]
    onesb = ctx["onesb"]

    wring = Ring([ar.v(AH + i * 256, 128, BF16) for i in range(16)])
    s1f = ar.v(AH + 4096, 512, F32)
    s2f = ar.v(AH + 6144, 512, F32)
    qbanks = [psum[:, 0:512], psum[:, 512:1024]]
    qfree = [None, None]
    for co in range(NC):
        wtoks = []
        wviews = []
        for ci in range(NC):
            wi, wv_, wf = wring.take()
            sch.wait("gpsimd", wf)
            wl = sch.dma("gpsimd", wv_, d["wqkvTqt"][ci, co], stream=f"bw{co % 2}")
            wviews.append((wi, wv_))
            wtoks.append(wl)
        sch.wait_all("tensor", wtoks)
        lastmm = None
        for k in range(NT):
            tsl = slice(512 * k, 512 * (k + 1))
            bk = qbanks[k % 2]
            sch.wait("tensor", qfree[k % 2])
            for ci in range(NC):
                lastmm = sch.op("tensor", lambda e, o=bk, l=wviews[ci][1],
                                r=xn[ci][:, tsl], st=(ci == 0), sp=(ci == NC - 1):
                                e.matmul(o, l, r, start=st, stop=sp),
                                inc="t" if ci == NC - 1 else None)
            sch.wait("vector", lastmm)
            v1 = sch.op("vector", lambda e, o=s1f, i=bk: e.tensor_scalar(
                o, i, 0.0, None, ALU.min), inc="v")
            sch.wait("scalar", v1)
            se = sch.op("scalar", lambda e, o=s2f, i=s1f: e.activation(
                o, i, AF.Exp, bias=0.0, scale=1.0), inc="s")
            sch.wait("vector", se)
            vphi = sch.op("vector", lambda e, o=qphi[co][:, tsl], i0=bk,
                          i1=s2f: e.scalar_tensor_tensor(
                              o, i0, 0.0, i1, ALU.max, ALU.add), inc="v")
            qfree[k % 2] = vphi
        for wi, _ in wviews:
            wring.free[wi] = lastmm

    wkv = [ar.v(AH + 8192 + ci * 4096, 2 * D, BF16) for ci in range(NC)]
    kvtoks = []
    for ci in range(NC):
        kvtoks.append(sch.dma("gpsimd", wkv[ci], d["wqkvTkv"][ci], stream="bkv"))
    sch.wait_all("tensor", kvtoks)
    kf1 = ar.v(AH + 40960, D, F32)
    kf2 = ar.v(AH + 45056, D, F32)
    kphi_t = [ar.v(AH + 49152, D, BF16), ar.v(AH + 57344, D, BF16)]
    vtile = [ar.v(AH + 51200, D, BF16), ar.v(AH + 59392, D, BF16)]
    ksrow = ar.v(AH + 53248, D, F32, p0=0, p1=1)
    kps = psum[:, 0:1024]
    vps = psum[:, 1024:2048]
    kvps = psum[:, 2048:3072]
    ksps = psum[0:1, 3072:4096]
    evac_tok = [qfree[0], qfree[1]]
    kv_last = None
    ks_last = None
    phi_free = [None, None]
    for kb in range(NQ):
        ksl = slice(128 * kb, 128 * (kb + 1))
        sl2 = kb % 2
        sch.wait("tensor", evac_tok)
        kmm = None
        for piece in range(2):
            for ci in range(NC):
                kmm = sch.op("tensor", lambda e,
                             o=psum[:, 512 * piece:512 * (piece + 1)],
                             l=xn[ci][:, ksl],
                             r=wkv[ci][:, 512 * piece:512 * (piece + 1)],
                             st=(ci == 0), sp=(ci == NC - 1):
                             e.matmul(o, l, r, start=st, stop=sp),
                             inc="t" if (ci == NC - 1 and piece == 1) else None)
        vmm = None
        for piece in range(2, 4):
            for ci in range(NC):
                vmm = sch.op("tensor", lambda e,
                             o=psum[:, 512 * piece:512 * (piece + 1)],
                             l=xn[ci][:, ksl],
                             r=wkv[ci][:, 512 * piece:512 * (piece + 1)],
                             st=(ci == 0), sp=(ci == NC - 1):
                             e.matmul(o, l, r, start=st, stop=sp),
                             inc="t" if (ci == NC - 1 and piece == 3) else None)
        sch.wait("vector", kmm)
        v1 = sch.op("vector", lambda e, o=kf1, i=kps: e.tensor_scalar(
            o, i, 0.0, None, ALU.min), inc="v")
        sch.wait("scalar", v1)
        se = sch.op("scalar", lambda e, o=kf2, i=kf1: e.activation(
            o, i, AF.Exp, bias=0.0, scale=1.0), inc="s")
        sch.wait("vector", [se, phi_free[sl2]])
        vphi = sch.op("vector", lambda e, o=kphi_t[sl2], i0=kps, i1=kf2:
                      e.scalar_tensor_tensor(o, i0, 0.0, i1, ALU.max, ALU.add),
                      inc="v")
        sch.wait("vector", vmm)
        ev = sch.op("vector", lambda e, o=vtile[sl2], i=vps: e.tensor_copy(o, i),
                    inc="v")
        evac_tok = [vphi, ev]
        sch.wait("tensor", ev)
        for c in range(NC):
            kv_last = sch.op("tensor", lambda e,
                             o=kvps[:, 128 * c:128 * (c + 1)],
                             l=kphi_t[sl2][:, 128 * c:128 * (c + 1)],
                             r=vtile[sl2][:, 128 * c:128 * (c + 1)],
                             st=(kb == 0 and c % 4 == 0),
                             sp=(kb == NQ - 1 and c % 4 == 3):
                             e.matmul(o, l, r, start=st, stop=sp),
                             inc="t" if c == NC - 1 else None)
        for hh in range(2):
            ks_last = sch.op("tensor", lambda e,
                             o=ksps[:, 512 * hh:512 * (hh + 1)],
                             l=onesb[:, 0:1],
                             r=kphi_t[sl2][:, 512 * hh:512 * (hh + 1)],
                             st=(kb == 0), sp=(kb == NQ - 1):
                             e.matmul(o, l, r, start=st, stop=sp),
                             inc="t" if hh == 1 else None)
        phi_free[sl2] = [kv_last, ks_last]

    wout_sb = [ar.v(OFF_B + 24576 + ci * 2048, D, BF16) for ci in range(NC)]
    wotoks = []
    sch.wait("gpsimd", [kv_last, ks_last])
    for ci in range(NC):
        wotoks.append(sch.dma("gpsimd", wout_sb[ci], d["woutTc"][ci], stream="bwo"))

    sch.wait("vector", ks_last)
    vk = sch.op("vector", lambda e, o=ksrow, i=ksps: e.tensor_copy(o, i), inc="v")
    ksb = ctx["vecs"][:, 20:28]
    sch.wait("tensor", vk)
    ttoks = []
    for s in range(NC):
        tt = sch.op("tensor", lambda e, o=psum[:, 3072 + s:3072 + s + 1],
                    i=ksrow[0:1, 128 * s:128 * (s + 1)],
                    idn=ctx["onesrow"][0:1, 0:1]:
                    e.transpose(o, i, idn), inc="t")
        ttoks.append(tt)
    sch.wait_all("vector", ttoks)
    vk2 = sch.op("vector", lambda e, o=ksb, i=psum[:, 3072:3080]:
                 e.tensor_copy(o, i), inc="v")
    sch.wait("vector", vk2)
    blk = ctx["blk"]
    btoks = []
    for c in range(NC):
        sch.op("vector", lambda e, o=blk[0:64, 16 * c + 2 * c:16 * c + 2 * c + 1],
               i=ksb[0:64, c:c + 1]: e.tensor_copy(o, i))
        bt = sch.op("vector", lambda e,
                    o=blk[64:128, 16 * c + 2 * c + 1:16 * c + 2 * c + 2],
                    i=ksb[64:128, c:c + 1]: e.tensor_copy(o, i), inc="v")
        btoks.append(bt)

    zden = ar.v(OFF_B + 0, T, F32, p0=0, p1=16)
    zbank2 = [psum[0:16, 0:512], psum[0:16, 512:1024]]
    sch.wait_all("tensor", btoks)
    zev = [None, None]
    for k in range(NT):
        tsl = slice(512 * k, 512 * (k + 1))
        zi = k % 2
        sch.wait("tensor", zev[zi])
        lastz = None
        for c in range(NC):
            lastz = sch.op("tensor", lambda e, o=zbank2[zi],
                           l=blk[:, 16 * c:16 * (c + 1)], r=qphi[c][:, tsl],
                           st=(c == 0), sp=(c == NC - 1):
                           e.matmul(o, l, r, start=st, stop=sp),
                           inc="t" if c == NC - 1 else None)
        sch.wait("vector", lastz)
        zev[zi] = sch.op("vector", lambda e, o=zden[0:16, tsl], i=zbank2[zi]:
                         e.tensor_copy(o, i), inc="v")
    v5 = sch.op("vector", lambda e, o=zden, i=zden: e.tensor_scalar(
        o, i, 1e-6, None, ALU.add), inc="v")
    sch.wait("scalar", v5)
    sch.op("scalar", lambda e, o=zden, i=zden: e.activation(
        o, i, AF.Ln, bias=0.0, scale=1.0))
    v6 = sch.op("scalar", lambda e, o=zden, i=zden: e.activation(
        o, i, AF.Exp, bias=0.0, scale=-1.0), inc="s")

    sch.wait("vector", kv_last)
    kvt = sch.op("vector", lambda e, o=ctx["kvsb"], i=kvps: e.tensor_copy(o, i),
                 inc="v")

    zbankA = [psum[:, 0:512], psum[:, 512:1024]]
    zbankB = [psum[:, 1024:1536], psum[:, 1536:2048]]
    zbs = [ar.v(OFF_B + 20480 + i * 2048, 512, F32) for i in range(2)]
    zfree = [None, None]
    zsfree = [None, None]
    sch.wait_all("tensor", [v6, kvt])
    for c in range(NC):
        for k in range(NT):
            tsl = slice(512 * k, 512 * (k + 1))
            i2 = (c * NT + k) % 2
            ba = zbankA[i2]
            bb2 = zbankB[i2]
            sch.wait("tensor", zfree[i2])
            zm = sch.op("tensor", lambda e, o=ba, l=ctx["indsb"][:, 128 * c:128 * (c + 1)],
                        r=zden[0:16, tsl]: e.matmul(o, l, r, start=True, stop=True),
                        inc="t")
            sch.op("tensor", lambda e, o=bb2[0:64, :],
                   l=ctx["kvsb"][0:64, 128 * c:128 * c + 64],
                   r=qphi[c][0:64, tsl]:
                   e.matmul(o, l, r, start=True, stop=True))
            mm2 = sch.op("tensor", lambda e, o=bb2[64:128, :],
                         l=ctx["kvsb"][64:128, 128 * c + 64:128 * c + 128],
                         r=qphi[c][64:128, tsl], tp=(64, 64):
                         e.matmul(o, l, r, start=True, stop=True, tile_position=tp),
                         inc="t")
            sch.wait("scalar", zm)
            sch.wait("scalar", zsfree[i2])
            zcp = sch.op("scalar", lambda e, o=zbs[i2], i=ba: e.copy(o, i), inc="s")
            sch.wait("vector", mm2)
            sch.wait("vector", zcp)
            vm = sch.op("vector", lambda e, o=mixer[c][:, tsl], a=bb2, b=zbs[i2]:
                        e.tensor_tensor(o, a, b, ALU.mult), inc="v")
            zfree[i2] = [vm, zcp]
            zsfree[i2] = vm

    sch.wait_all("tensor", wotoks)
    xring = Ring([ar.v(OFF_B + 45056 + i * 2048, 512, F32) for i in range(2)])
    orng = Ring([ar.v(OFF_B + 49152 + i * 2048, 512, F32) for i in range(2)])
    xring.free = [vm, vm]
    orng.free = [vm, vm]
    evac_done = [[zfree[0], zfree[1]], None]
    for k in range(NT):
        tsl = slice(512 * k, 512 * (k + 1))
        for half in range(2):
            sch.wait("tensor", evac_done[half])
            mmt = None
            for ci in range(NC):
                for co2 in range(4):
                    co = 4 * half + co2
                    mmt = sch.op("tensor", lambda e,
                                 o=psum[:, 512 * co:512 * (co + 1)],
                                 l=wout_sb[ci][:, 128 * co:128 * (co + 1)],
                                 r=mixer[ci][:, tsl], st=(ci == 0),
                                 sp=(ci == NC - 1):
                                 e.matmul(o, l, r, start=st, stop=sp),
                                 inc="t" if (ci == NC - 1 and co2 == 3) else None)
            va = None
            for co2 in range(4):
                co = 4 * half + co2
                ri, xv, xf = xring.take()
                sch.wait("sync", xf)
                ld = sch.dma("sync", xv, xsrc[co][:, tsl], stream=f"bx{ri}")
                oi, ov, of = orng.take()
                sch.wait("vector", [of, ld, mmt])
                va = sch.op("vector", lambda e, o=ov,
                            i0=psum[:, 512 * co:512 * (co + 1)],
                            s=ctx["boutBsb"][:, co:co + 1], i1=xv:
                            e.scalar_tensor_tensor(o, i0, s, i1, ALU.add, ALU.add),
                            inc="v")
                xring.release(ri, va)
                sch.wait("sync", va)
                st = sch.dma("sync", xdst[co][:, tsl], ov, stream=f"bo{oi}")
                orng.release(oi, st)
            evac_done[half] = va


def emit_debug_xn(ctx):
    sch, ar = ctx["sch"], ctx["ar"]
    A = OFF_A
    tmps = Ring([ar.v(A + 24576 + i * 2048, 512, F32) for i in range(2)])
    for c in range(NC):
        for k in range(NT):
            xn_t = ar.v(OFF_B + c * T * 2 + 1024 * k, 512, BF16)
            ti, tv, tf = tmps.take()
            sch.wait("vector", tf)
            cp = sch.op("vector", lambda e, o=tv, i=xn_t: e.tensor_copy(o, i), inc="v")
            sch.wait("sync", cp)
            st = sch.dma("sync", ctx["d"]["y_out"]
                         [c * 128:(c + 1) * 128, 512 * k:512 * (k + 1)], tv,
                         stream=f"dbg{ti}")
            tmps.release(ti, st)
    sch.wait_all("gpsimd", sch.barrier_tokens())


def emit_debug_hbm(ctx, chunks):
    sch, ar = ctx["sch"], ctx["ar"]
    tmps = Ring([ar.v(OFF_A + i * 16384, T, F32) for i in range(2)])
    for c in range(NC):
        ti, tv, tf = tmps.take()
        sch.wait("sync", tf)
        ld = sch.dma("sync", tv, chunks[c], stream=f"dbg{ti}")
        sch.wait("sync", ld)
        st = sch.dma("sync", ctx["d"]["y_out"][c * 128:(c + 1) * 128, :], tv, stream=f"dbg{ti}")
        tmps.release(ti, st)
    sch.wait_all("gpsimd", sch.barrier_tokens())
    sch.wait_all("sync", sch.barrier_tokens())



def kernel(**inputs):
    cores = _host_prep(inputs)
    nc = build_program()
    res = run_bass_kernel_spmd(nc, cores, core_ids=list(range(B)))
    out = np.empty((B, T, D), dtype=np.float32)
    for i in range(B):
        out[i] = res.results[i]["y_out"].T
    return out
